# version 03
# baseline (speedup 1.0000x reference)
import numpy as np
import ml_dtypes
from contextlib import ExitStack

import concourse.bass as bass
import concourse.mybir as mybir
from concourse.bass import IndirectOffsetOnAxis
from concourse.bass_utils import run_bass_kernel_spmd

F32 = mybir.dt.float32
BF16 = mybir.dt.bfloat16
I32 = mybir.dt.int32
AF = mybir.ActivationFunctionType
ALU = mybir.AluOpType
AX = mybir.AxisListType

MAGIC = 12582912.0
BIG = 32768.0
TOK, NT, SEQ, NQT = 2048, 16, 8192, 64
DM, I4, QCD, HD, NB = 1024, 4096, 256, 64, 256
BLK = 4

SEC_QL, SEC_K, SEC_V, SEC_KC, SEC_VC, SEC_K2 = 0, 4096, 5120, 6144, 6208, 6272
NCB = 6336
G2R = 8192
RG = [[0, 1, 2, 3], [4, 5, 6, 7]]

DEBUG_OUTS = []
TRUNC = None


class _Done(Exception):
    pass


def host_consts():
    c = {}
    c["IDENT"] = np.eye(128, dtype=ml_dtypes.bfloat16)
    c["IDENT32"] = np.eye(128, dtype=np.float32)
    c["ONES1x128_32"] = np.ones((1, 128), np.float32)
    k = np.arange(SEQ)
    c["R32BIG"] = (BIG * ((k[None, :] // 32) % 32 == np.arange(32)[:, None])
                   ).astype(ml_dtypes.bfloat16)
    c["ONESROW"] = np.ones((1, SEQ), ml_dtypes.bfloat16)
    c["ZROW"] = np.zeros((1, SEQ), ml_dtypes.bfloat16)
    c["Z32R"] = np.zeros((32, SEQ), ml_dtypes.bfloat16)
    e65 = np.zeros((1, 65), ml_dtypes.bfloat16); e65[0, 64] = 1.0
    c["E65"] = e65
    r = np.arange(128)
    c["CAUSAL_ADD"] = np.where(r[None, :] >= r[:, None], 0.0, -BIG
                               ).astype(ml_dtypes.bfloat16)
    c["WINEDGE_ADD"] = np.where(r[None, :] < r[:, None], 0.0, -BIG
                                ).astype(ml_dtypes.bfloat16)
    c["FULLNEG"] = np.full((128, 128), -BIG, ml_dtypes.bfloat16)
    c["B32AVG"] = ((r[:, None] // 32 == np.arange(4)[None, :]) / 32.0
                   ).astype(ml_dtypes.bfloat16)
    n = np.arange(NB)
    q = np.arange(SEQ)
    c["VADDT"] = np.where((32 * n[:, None] + 31) > q[None, :], -BIG, 0.0
                          ).astype(ml_dtypes.bfloat16)
    vq = np.empty((128, NQT * NB), ml_dtypes.bfloat16)
    for qt in range(NQT):
        qv = qt * 128 + r
        vq[:, qt * NB:(qt + 1) * NB] = np.where(
            (32 * n[None, :] + 31) > qv[:, None], -BIG, 0.0)
    c["VADDQ"] = vq
    c["NLT16M1"] = np.broadcast_to((n < 16).astype(np.float32) - 1.0,
                                   (128, NB)).astype(ml_dtypes.bfloat16).copy()
    z31r = np.ones((1, 128), np.float32); z31r[0, :31] = 0.0
    c["Z31R"] = z31r
    half = 16
    inv = 1.0 / (10000.0 ** (np.arange(half, dtype=np.float64) * 2.0 / 32.0))
    ang = np.arange(SEQ, dtype=np.float64)[:, None] * inv[None, :]
    cosf = np.cos(ang).astype(np.float32)
    sinf = np.sin(ang).astype(np.float32)
    c["RCOSF"] = np.ascontiguousarray(
        cosf.reshape(NQT, 128, half).transpose(1, 0, 2).reshape(128, NQT * half))
    c["RSINF"] = np.ascontiguousarray(
        sinf.reshape(NQT, 128, half).transpose(1, 0, 2).reshape(128, NQT * half))
    return c, cosf, sinf


def build_program(debug=()):
    nc = bass.Bass("TRN2", target_bir_lowering=False)
    dbg = set(debug)

    def din(name, shape, dt=F32):
        return nc.dram_tensor(name, shape, dt, kind="ExternalInput").ap()

    XIN = din("XIN", [TOK, DM])
    WQC = din("WQC", [DM, QCD])
    WQUPM = din("WQUPM", [QCD, 128])
    WK = din("WK", [DM, HD]); WV = din("WV", [DM, HD])
    GQN = din("GQN", [1, HD]); GKN = din("GKN", [1, HD])
    WGATE = din("WGATE", [HD, 3]); BGATE = din("BGATE", [1, 3])
    WOG = din("WOG", [2, QCD, 512])
    WOUT = din("WOUT", [DM, DM])
    WGT = din("WGT", [DM, I4]); WUT = din("WUT", [DM, I4])
    WDT = din("WDT", [I4, DM])
    MYSINK = din("MYSINK", [1, 2])
    IDENT = din("IDENT", [128, 128], BF16)
    IDENT32 = din("IDENT32", [128, 128], F32)
    ONES1x128_32C = din("ONES1x128_32", [1, 128], F32)
    R32BIGC = din("R32BIG", [32, SEQ], BF16)
    ONESROWC = din("ONESROW", [1, SEQ], BF16)
    ZROWC = din("ZROW", [1, SEQ], BF16)
    Z32RC = din("Z32R", [32, SEQ], BF16)
    E65C = din("E65", [1, 65], BF16)
    CAUSC = din("CAUSAL_ADD", [128, 128], BF16)
    WINEC = din("WINEDGE_ADD", [128, 128], BF16)
    FULLNEGC = din("FULLNEG", [128, 128], BF16)
    B32C = din("B32AVG", [128, 4], BF16)
    VADDTC = din("VADDT", [NB, SEQ], BF16)
    VADDQC = din("VADDQ", [128, NQT * NB], BF16)
    NLT16C = din("NLT16M1", [128, NB], BF16)
    Z31RC = din("Z31R", [1, 128], F32)
    RCOSF = din("RCOSF", [128, NQT * 16]); RSINF = din("RSINF", [128, NQT * 16])
    RCOSS = din("RCOSS", [128, NT * 16]); RSINS = din("RSINS", [128, NT * 16])
    G2IDXC = din("G2IDX", [128, 4], I32)

    OUT = nc.dram_tensor("OUT", [TOK, DM], F32, kind="ExternalOutput").ap()
    dbg_t = {}
    def dout(name, shape, dt=F32):
        dbg_t[name] = nc.dram_tensor("DBG_" + name, shape, dt,
                                     kind="ExternalOutput").ap()
        return dbg_t[name]

    WGC = nc.dram_tensor("WGC", [DM, I4], BF16).ap()
    WUC = nc.dram_tensor("WUC", [DM, I4], BF16).ap()
    WDC = nc.dram_tensor("WDC", [I4, DM], BF16).ap()
    X1D = nc.dram_tensor("X1D", [TOK, DM], F32).ap()
    X2D = nc.dram_tensor("X2D", [TOK, DM], F32).ap()
    CONTRIB = nc.dram_tensor("CONTRIB", [NCB, 128], BF16).ap()
    GOUT = nc.dram_tensor("GOUT", [4 * NCB, 128], BF16).ap()
    G2C = nc.dram_tensor("G2C", [G2R, 128], BF16).ap()
    G2OUT = nc.dram_tensor("G2OUT", [4 * G2R, 128], BF16).ap()
    MROWD = nc.dram_tensor("MROWD", [2, SEQ], BF16).ap()

    ENGS = ("sync", "tensor", "vector", "scalar", "gpsimd")
    plan = {e: [] for e in ENGS}

    class EngTrack:
        _SKIP = {"wait_ge", "nop", "maybe_wait_ge", "wait_op", "drain"}
        _NOSMALL = {"dma_start", "collective_compute", "indirect_dma_start",
                    "matmul", "transpose", "ldweights", "memset", "memzero"}
        _FENCE = {"reduce_sum", "reduce_max", "tensor_reduce", "max",
                  "reciprocal", "bn_stats", "bn_aggr", "activation"}

        def __init__(self, eng, fence_sem):
            self._e = eng
            self._last = None
            self._fs = fence_sem
            self._fn = 0

        def fence(self):
            self._e.drain()

        def __getattr__(self, name):
            f = getattr(self._e, name)
            if callable(f) and not name.startswith("_"):
                track = name not in EngTrack._SKIP
                do_fence = name in EngTrack._FENCE

                def wrap(*a, _f=f, _track=track, _df=do_fence, _n=name, **k):
                    r = _f(*a, **k)
                    if r is not None and hasattr(r, "then_inc"):
                        self._last = r
                        if _track:
                            df = _df
                            if not df and _n not in EngTrack._NOSMALL:
                                out = a[0] if a else k.get("out")
                                shp = getattr(out, "shape", None)
                                if shp is not None and len(shp) >= 2:
                                    fe = 1
                                    for d_ in shp[1:]:
                                        fe *= d_
                                    df = fe <= 16
                            if df:
                                self.fence()
                    return r
                return wrap
            return f

        def inc_last(self, sem, n=1):
            self._last.then_inc(sem, n)

    def on(e):
        def deco(fn):
            plan[e].append(fn)
        return deco

    class Ctr:
        __slots__ = ("h", "n")
        def __init__(self, h): self.h = h; self.n = 0
        def inc(self, k=1): self.n += k; return self.n

    root = ExitStack()
    def sem(name):
        return Ctr(root.enter_context(nc.semaphore(name)))

    def sbuf(scope, name, shape, dt=F32):
        return scope.enter_context(nc.sbuf_tensor(name, shape, dt))

    def psum(scope, name, shape, dt=F32):
        return scope.enter_context(nc.psum_tensor(name, shape, dt))

    sIN = sem("sIN"); sSC = sem("sSC"); sGP = sem("sGP")
    sPE = sem("sPE"); sDV = sem("sDV"); sAC = sem("sAC"); sPL = sem("sPL")
    sOD = sem("sOD")
    sBAR = sem("sBAR")

    def barrier():
        t = sBAR.inc(5)
        for e in ENGS:
            @on(e)
            def _(eng, t=t):
                eng.drain().then_inc(sBAR.h, 1)
                eng.wait_ge(sBAR.h, t)

    P = ExitStack()
    ident = sbuf(P, "ident", [128, 128], BF16)
    ident32 = sbuf(P, "ident32", [128, 128], F32)
    ones1x128_32 = sbuf(P, "ones1x128_32", [1, 128], F32)
    causadd = sbuf(P, "causadd", [128, 128], BF16)
    wineadd = sbuf(P, "wineadd", [128, 128], BF16)
    fullneg = sbuf(P, "fullneg", [128, 128], BF16)
    b32avg = sbuf(P, "b32avg", [128, 4], BF16)
    nlt16 = sbuf(P, "nlt16", [128, NB], BF16)
    z31r = sbuf(P, "z31r", [1, 128], F32)
    e65 = sbuf(P, "e65", [1, 65], BF16)
    gqv = sbuf(P, "gqv", [1, HD]); gkv = sbuf(P, "gkv", [1, HD])
    sinks = sbuf(P, "sinks", [1, 2])
    rcosf = sbuf(P, "rcosf", [128, NQT * 16]); rsinf = sbuf(P, "rsinf", [128, NQT * 16])
    rcoss = sbuf(P, "rcoss", [128, NT * 16]); rsins = sbuf(P, "rsins", [128, NT * 16])
    g2idx = sbuf(P, "g2idx", [128, 4], I32)
    wqcb = sbuf(P, "wqcb", [128, 8 * QCD], BF16)
    wqupb = sbuf(P, "wqupb", [128, 2 * 128], BF16)
    wkb = sbuf(P, "wkb", [128, 8 * HD], BF16)
    wvb = sbuf(P, "wvb", [128, 8 * HD], BF16)
    wogb = sbuf(P, "wogb", [128, 4 * 512], BF16)
    wg98 = sbuf(P, "wg98", [98, 3], BF16)
    kmaxv = sbuf(P, "kmaxv", [1, 1])
    mwg = sbuf(P, "mwg", [1, 1]); mwu = sbuf(P, "mwu", [1, 1]); mwd = sbuf(P, "mwd", [1, 1])
    kgu = sbuf(P, "kgu", [128, 1]); kdn = sbuf(P, "kdn", [128, 1])
    gq128 = sbuf(P, "gq128", [128, HD])
    gk128 = sbuf(P, "gk128", [128, HD])
    onescol32 = sbuf(P, "onescol32", [128, 1])
    m1c = sbuf(P, "m1c", [128, 1]); v1c = sbuf(P, "v1c", [128, 1])
    s1c = sbuf(P, "s1c", [128, 1]); t11 = sbuf(P, "t11", [1, 1])
    t128r = sbuf(P, "t128r", [1, 128])
    x0t = [sbuf(P, "x0t0", [128, DM]), sbuf(P, "x0t1", [128, DM])]
    xln = sbuf(P, "xln", [128, DM])
    lnsc = sbuf(P, "lnsc", [128, DM])

    _init_loads = [
        (ident[:], IDENT), (ident32[:], IDENT32), (ones1x128_32[:], ONES1x128_32C),
        (causadd[:], CAUSC), (wineadd[:], WINEC), (fullneg[:], FULLNEGC),
        (b32avg[:], B32C), (nlt16[:], NLT16C), (z31r[:], Z31RC), (e65[:], E65C),
        (gqv[:], GQN), (gkv[:], GKN), (sinks[:], MYSINK),
        (rcosf[:], RCOSF), (rsinf[:], RSINF), (rcoss[:], RCOSS), (rsins[:], RSINS),
        (g2idx[:], G2IDXC),
    ]
    import os as _os
    _init_loads = _init_loads[:int(_os.environ.get("K_INIT_N", "99"))]
    sIN.inc(16 * len(_init_loads))
    @on("sync")
    def _(s, items=_init_loads):
        for dst, src in items:
            s.dma_start(out=dst, in_=src).then_inc(sIN.h, 16)

    _gp_loads = [
        (wqcb[:].rearrange("p (a q) -> p a q", q=QCD),
         WQC.rearrange("(a p) q -> p a q", p=128)),
        (wqupb[:].rearrange("p (a q) -> p a q", q=128),
         WQUPM.rearrange("(a p) q -> p a q", p=128)),
        (wkb[:].rearrange("p (a q) -> p a q", q=HD),
         WK.rearrange("(a p) q -> p a q", p=128)),
        (wvb[:].rearrange("p (a q) -> p a q", q=HD),
         WV.rearrange("(a p) q -> p a q", p=128)),
        (wogb[:].rearrange("p (g a q) -> p g a q", g=2, q=512),
         WOG.rearrange("g (a p) q -> p g a q", p=128)),
        (wg98[0:64, :], WGATE),
        (wg98[97:98, :], BGATE),
    ]
    _gp_loads = _gp_loads[:int(_os.environ.get("K_GP_N", "99"))]
    sGP.inc(16 * len(_gp_loads))
    @on("gpsimd")
    def _(g, items=_gp_loads):
        for dst, src in items:
            g.dma_start(out=dst, in_=src).then_inc(sGP.h, 16)
    @on("vector")
    def _(v):
        v.memset(onescol32[:], 1.0).then_inc(sDV.h, 1)
    sDV.inc()
    import os as _osm
    if _osm.environ.get("K_NOMEMSET", "0") != "1":
        _w_gp0 = sGP.n
        @on("vector")
        def _(v, w=_w_gp0):
            v.wait_ge(sGP.h, w)
            v.memset(wg98[64:97, :], 0.0).then_inc(sDV.h, 1)
        sDV.inc()

    W = ExitStack()
    wqb = [sbuf(W, f"wqb{i}", [128, I4]) for i in range(2)]
    wcb = [sbuf(W, f"wcb{i}", [128, I4], BF16) for i in range(2)]
    asums = sbuf(W, "asums", [128, 32])
    asumt = sbuf(W, "asumt", [128, 1])
    wscal = sbuf(W, "wscal", [128, 1])
    wqscr = sbuf(W, "wqscr", [128, I4])
    p_t1 = psum(W, "p_t1", [1, 128])
    p_s11 = psum(W, "p_s11", [1, 1])
    p_bc = psum(W, "p_bc", [128, 1])
    p_gbq = psum(W, "p_gbq", [128, HD])
    p_gbk = psum(W, "p_gbk", [128, HD])

    import os as _osx
    if _osx.environ.get("K_NOBC", "0") != "1":
        _w_in0 = sIN.n
        _pw = sPE.inc()
        @on("tensor")
        def _(t, w=_w_in0):
            t.wait_ge(sIN.h, w)
            t.matmul(p_gbq[:], ones1x128_32[:], gqv[:], start=True, stop=True)
            t.matmul(p_gbk[:], ones1x128_32[:], gkv[:], start=True, stop=True)
            t.inc_last(sPE.h, 1)
        @on("vector")
        def _(v, w=_pw):
            v.wait_ge(sPE.h, w)
            v.tensor_copy(gq128[:], p_gbq[:])
            v.tensor_copy(gk128[:], p_gbk[:]).then_inc(sDV.h, 1)
        sDV.inc()

    def bcast_scalar(dst_col):
        w = sDV.n
        pw = sPE.inc()
        @on("tensor")
        def _(t, w=w):
            t.wait_ge(sDV.h, w)
            t.matmul(p_bc[:], ones1x128_32[:], t11[:], start=True, stop=True
                     ).then_inc(sPE.h, 1)
        @on("vector")
        def _(v, w=pw, dst=dst_col):
            v.wait_ge(sPE.h, w)
            v.tensor_copy(dst, p_bc[:]).then_inc(sDV.h, 1)
        sDV.inc()

    def quant_matrix(W_dram, C_dram, ntiles, ncols, mw_dst):
        for t in range(ntiles):
            dvw = sDV.n
            w = sIN.inc(16)
            @on("sync")
            def _(s, t=t, dvw=dvw):
                s.wait_ge(sDV.h, dvw)
                s.dma_start(out=wqb[t % 2][:, 0:ncols],
                            in_=W_dram[t * 128:(t + 1) * 128, :]).then_inc(sIN.h, 16)
            @on("vector")
            def _(v, t=t, w=w):
                v.wait_ge(sIN.h, w)
                v.reduce_sum(asums[:, t:t + 1], wqb[t % 2][:, 0:ncols], axis=AX.X,
                             apply_absolute_value=True).then_inc(sDV.h, 1)
            sDV.inc()
        @on("vector")
        def _(v, nt=ntiles):
            v.reduce_sum(asumt[:], asums[:, 0:nt], axis=AX.X).then_inc(sDV.h, 1)
        sDV.inc()
        w = sDV.n
        pw = sPE.inc()
        @on("tensor")
        def _(t, w=w):
            t.wait_ge(sDV.h, w)
            t.matmul(p_s11[:], asumt[:], onescol32[:],
                     start=True, stop=True).then_inc(sPE.h, 1)
        @on("vector")
        def _(v, w=pw, nel=float(ntiles * 128 * ncols), mw_dst=mw_dst):
            v.wait_ge(sPE.h, w)
            v.tensor_scalar(t11[:], p_s11[:], 1.0 / nel, 1e-5, ALU.mult, ALU.max)
            v.tensor_copy(mw_dst[:], t11[:])
            v.reciprocal(t11[:], t11[:]).then_inc(sDV.h, 1)
        sDV.inc()
        bcast_scalar(wscal[:])
        for t in range(ntiles):
            odw = sOD.n
            w = sIN.inc(16)
            @on("sync")
            def _(s, t=t, odw=odw):
                s.wait_ge(sOD.h, odw)
                s.dma_start(out=wqb[t % 2][:, 0:ncols],
                            in_=W_dram[t * 128:(t + 1) * 128, :]).then_inc(sIN.h, 16)
            @on("vector")
            def _(v, t=t, w=w, ncols=ncols):
                v.wait_ge(sIN.h, w)
                b = wqb[t % 2][:, 0:ncols]
                v.tensor_scalar(wqscr[:, 0:ncols], b, wscal[:, 0:1], MAGIC,
                                ALU.mult, ALU.add)
                v.tensor_scalar(wqscr[:, 0:ncols], wqscr[:, 0:ncols], MAGIC, 1.0,
                                ALU.subtract, ALU.min)
                v.tensor_scalar_max(wcb[t % 2][:, 0:ncols], wqscr[:, 0:ncols], -1.0
                                    ).then_inc(sDV.h, 1)
            sDV.inc()
            w2 = sDV.n
            sOD.inc(16)
            @on("sync")
            def _(s, t=t, w2=w2, ncols=ncols):
                s.wait_ge(sDV.h, w2)
                s.dma_start(out=C_dram[t * 128:(t + 1) * 128, :],
                            in_=wcb[t % 2][:, 0:ncols]).then_inc(sOD.h, 16)

    _scopes = []

    def _chk(tag):
        if TRUNC == tag:
            @on("sync")
            def _(se, a=sIN.n, b=sOD.n, c=sSC.n, d=sGP.n):
                se.wait_ge(sIN.h, a)
                se.wait_ge(sOD.h, b)
                se.wait_ge(sSC.h, c)
                se.wait_ge(sGP.h, d)
            for sc in reversed(_scopes):
                sc.close()
            raise _Done

    def _main():
        if TRUNC == "INIT":
            import os as _os2
            if _os2.environ.get("K_WRITEOUT", "0") == "1":
                sOD.inc(16)
                @on("sync")
                def _(s):
                    s.dma_start(out=OUT[0:128, :], in_=x0t[0][:]).then_inc(sOD.h, 16)
            W.close()
            raise _Done
        quant_matrix(WGT, WGC, 8, I4, mwg)
        quant_matrix(WUT, WUC, 8, I4, mwu)
        quant_matrix(WDT, WDC, 32, DM, mwd)
        @on("vector")
        def _(v):
            v.tensor_mul(t11[:], mwg[:], mwg[:])
            v.tensor_mul(t11[:], t11[:], mwu[:]).then_inc(sDV.h, 1)
        sDV.inc()
        bcast_scalar(kgu[:])
        @on("vector")
        def _(v):
            v.tensor_scalar_mul(t11[:], mwd[:], 0.5 / 127.0).then_inc(sDV.h, 1)
        sDV.inc()
        bcast_scalar(kdn[:])
        if "wq" in dbg:
            tw = dout("wgc", [DM, I4], BF16)
            ts_ = dout("scal", [1, 4])
            scl4 = sbuf(W, "scl4", [1, 4])
            @on("vector")
            def _(v):
                v.tensor_copy(scl4[0:1, 0:1], mwg[:])
                v.tensor_copy(scl4[0:1, 1:2], mwu[:])
                v.tensor_copy(scl4[0:1, 2:3], mwd[:]).then_inc(sDV.h, 1)
            sDV.inc()
            sOD.inc(32)
            @on("sync")
            def _(s, w=sDV.n, tw=tw, ts_=ts_):
                s.wait_ge(sDV.h, w)
                s.dma_start(out=tw, in_=WGC[:, :]).then_inc(sOD.h, 16)
                s.dma_start(out=ts_, in_=scl4[:]).then_inc(sOD.h, 16)
        barrier()
        W.close()
        _chk("W")

        def ffn_call(tag, xsrc, xdst):
            S = ExitStack()
            xq_t = [sbuf(S, f"{tag}xqt{i}", [128, DM], BF16) for i in range(BLK)]
            hbuf = sbuf(S, f"{tag}hbuf", [128, BLK * I4])
            hqt = sbuf(S, f"{tag}hqt", [128, BLK * I4], BF16)
            wgu = [sbuf(S, f"{tag}wgu{i}", [128, 2 * I4], BF16) for i in range(2)]
            wdb = [sbuf(S, f"{tag}wd{i}", [128, 512], BF16) for i in range(2)]
            alph = sbuf(S, f"{tag}alph", [128, BLK])
            beth = sbuf(S, f"{tag}beth", [128, BLK])
            hm8 = sbuf(S, f"{tag}hm8", [128, 8 * BLK])
            rsc = sbuf(S, f"{tag}rsc", [128, 512])
            r2sc = sbuf(S, f"{tag}r2sc", [128, 512])
            x1a = sbuf(S, f"{tag}x1a", [128, BLK * 512])
            hqc = sbuf(S, f"{tag}hqc", [128, I4], BF16)
            p_gu = [psum(S, f"{tag}pgu{i}", [128, 512]) for i in range(4)]
            p_tr = [psum(S, f"{tag}ptr{i}", [128, 128], BF16) for i in range(2)]

            for blk in range(4):
                tiles = [blk * BLK + i for i in range(BLK)]
                for i, ti in enumerate(tiles):
                    dvw = sDV.n
                    w = sIN.inc(16)
                    @on("sync")
                    def _(s, ti=ti, i=i, dvw=dvw):
                        s.wait_ge(sDV.h, dvw)
                        s.dma_start(out=x0t[i % 2][:],
                                    in_=xsrc[ti * 128:(ti + 1) * 128, :]).then_inc(sIN.h, 16)
                    aw = sAC.inc()
                    @on("scalar")
                    def _(a, w=w, i=i):
                        a.wait_ge(sIN.h, w)
                        a.activation(lnsc[:], x0t[i % 2][:], AF.Square, accum_out=v1c[:]
                                     ).then_inc(sAC.h, 1)
                    @on("vector")
                    def _(v, w=w, aw=aw, i=i):
                        v.wait_ge(sIN.h, w)
                        v.reduce_sum(m1c[:], x0t[i % 2][:], axis=AX.X)
                        v.tensor_scalar_mul(m1c[:], m1c[:], 1.0 / DM)
                        v.wait_ge(sAC.h, aw)
                        v.tensor_scalar_mul(v1c[:], v1c[:], 1.0 / DM)
                        v.tensor_mul(s1c[:], m1c[:], m1c[:])
                        v.tensor_sub(v1c[:], v1c[:], s1c[:])
                        v.tensor_scalar_add(v1c[:], v1c[:], 1e-5).then_inc(sDV.h, 1)
                    sDV.inc()
                    aw = sAC.inc(2)
                    @on("scalar")
                    def _(a, w=sDV.n):
                        a.wait_ge(sDV.h, w)
                        a.activation(v1c[:], v1c[:], AF.Ln)
                        a.activation(s1c[:], v1c[:], AF.Exp, scale=-0.5).then_inc(sAC.h, 2)
                    @on("vector")
                    def _(v, aw=aw, i=i):
                        v.wait_ge(sAC.h, aw)
                        v.tensor_scalar(xln[:], x0t[i % 2][:], m1c[:, 0:1], None,
                                        ALU.subtract)
                        v.tensor_scalar_mul(xln[:], xln[:], s1c[:, 0:1])
                        v.reduce_max(m1c[:], xln[:], axis=AX.X, apply_absolute_value=True)
                        v.tensor_scalar_max(m1c[:], m1c[:], 1e-5)
                        v.reciprocal(s1c[:], m1c[:])
                        v.tensor_scalar_mul(s1c[:], s1c[:], 127.0)
                        v.tensor_scalar_mul(v1c[:], m1c[:], 1.0 / 127.0)
                        v.tensor_mul(alph[:, i:i + 1], v1c[:], v1c[:])
                        v.tensor_mul(alph[:, i:i + 1], alph[:, i:i + 1], v1c[:])
                        v.tensor_mul(alph[:, i:i + 1], alph[:, i:i + 1], kgu[:, 0:1])
                        v.tensor_scalar(xln[:], xln[:], s1c[:, 0:1], MAGIC, ALU.mult, ALU.add)
                        v.tensor_scalar(xln[:], xln[:], MAGIC, 127.0, ALU.subtract, ALU.min)
                        v.tensor_scalar_max(hqc[:, 0:DM], xln[:], -128.0).then_inc(sDV.h, 1)
                    sDV.inc()
                    w2 = sDV.n
                    for k in range(8):
                        dvw = sDV.n
                        pw = sPE.inc()
                        @on("tensor")
                        def _(t, w2=w2, k=k, dvw=dvw):
                            t.wait_ge(sDV.h, max(w2, dvw - 1))
                            t.transpose(p_tr[k % 2][:], hqc[:, k * 128:(k + 1) * 128],
                                        ident[:]).then_inc(sPE.h, 1)
                        @on("vector")
                        def _(v, w=pw, i=i, k=k):
                            v.wait_ge(sPE.h, w)
                            v.tensor_copy(xq_t[i][:, k * 128:(k + 1) * 128],
                                          p_tr[k % 2][:]).then_inc(sDV.h, 1)
                        sDV.inc()
                xq_ready = sDV.n
                _hdv = []
                for ic in range(8):
                    pew = sPE.n
                    wld = sIN.inc(32)
                    @on("sync")
                    def _(s, ic=ic, pew=pew):
                        s.wait_ge(sPE.h, pew)
                        s.dma_start(
                            out=wgu[ic % 2][:, 0:I4].rearrange("p (a q) -> p a q", q=512),
                            in_=WGC.rearrange("(a p) q -> p a q", p=128)[
                                :, :, ic * 512:(ic + 1) * 512]).then_inc(sIN.h, 16)
                        s.dma_start(
                            out=wgu[ic % 2][:, I4:2 * I4].rearrange("p (a q) -> p a q", q=512),
                            in_=WUC.rearrange("(a p) q -> p a q", p=128)[
                                :, :, ic * 512:(ic + 1) * 512]).then_inc(sIN.h, 16)
                    for i in range(BLK):
                        par = (ic * BLK + i) % 2
                        pg, pu = p_gu[2 * par], p_gu[2 * par + 1]
                        gidx = ic * BLK + i
                        dvw = _hdv[gidx - 2] if gidx >= 2 else xq_ready
                        pw = sPE.inc()
                        @on("tensor")
                        def _(t, ic=ic, i=i, pg=pg, pu=pu, wld=wld, dvw=dvw,
                              xqr=xq_ready):
                            t.wait_ge(sIN.h, wld)
                            t.wait_ge(sDV.h, max(dvw, xqr))
                            for k in range(8):
                                t.matmul(pg[:], xq_t[i][:, k * 128:(k + 1) * 128],
                                         wgu[ic % 2][:, k * 512:(k + 1) * 512],
                                         start=(k == 0), stop=(k == 7))
                            for k in range(8):
                                t.matmul(pu[:], xq_t[i][:, k * 128:(k + 1) * 128],
                                         wgu[ic % 2][:, I4 + k * 512:I4 + (k + 1) * 512],
                                         start=(k == 0), stop=(k == 7))
                            t.inc_last(sPE.h, 1)
                        aw = sAC.inc()
                        @on("scalar")
                        def _(a, w=pw, pg=pg):
                            a.wait_ge(sPE.h, w)
                            a.activation(rsc[:], pg[:], AF.Relu).then_inc(sAC.h, 1)
                        @on("vector")
                        def _(v, aw=aw, i=i, ic=ic, pu=pu):
                            v.wait_ge(sAC.h, aw)
                            v.tensor_mul(r2sc[:], rsc[:], rsc[:])
                            v.scalar_tensor_tensor(
                                hbuf[:, i * I4 + ic * 512:i * I4 + (ic + 1) * 512],
                                pu[:], alph[:, i:i + 1], r2sc[:], ALU.mult, ALU.mult)
                            v.reduce_max(hm8[:, i * 8 + ic:i * 8 + ic + 1],
                                         hbuf[:, i * I4 + ic * 512:i * I4 + (ic + 1) * 512],
                                         axis=AX.X, apply_absolute_value=True
                                         ).then_inc(sDV.h, 1)
                        sDV.inc()
                        _hdv.append(sDV.n)
                for i in range(BLK):
                    @on("vector")
                    def _(v, i=i):
                        v.reduce_max(m1c[:], hm8[:, i * 8:(i + 1) * 8], axis=AX.X)
                        v.tensor_scalar_max(m1c[:], m1c[:], 1e-5)
                        v.reciprocal(s1c[:], m1c[:])
                        v.tensor_scalar_mul(s1c[:], s1c[:], 127.0)
                        v.tensor_mul(beth[:, i:i + 1], m1c[:], kdn[:, 0:1])
                        h = hbuf[:, i * I4:(i + 1) * I4]
                        v.tensor_scalar(h, h, s1c[:, 0:1], MAGIC, ALU.mult, ALU.add)
                        v.tensor_scalar(h, h, MAGIC, 127.0, ALU.subtract, ALU.min)
                        v.tensor_scalar_max(hqc[:], h, -128.0).then_inc(sDV.h, 1)
                    sDV.inc()
                    w2 = sDV.n
                    for k in range(32):
                        dvw = sDV.n
                        pw = sPE.inc()
                        @on("tensor")
                        def _(t, w2=w2, k=k, dvw=dvw):
                            t.wait_ge(sDV.h, max(w2, dvw - 1))
                            t.transpose(p_tr[k % 2][:], hqc[:, k * 128:(k + 1) * 128],
                                        ident[:]).then_inc(sPE.h, 1)
                        @on("vector")
                        def _(v, w=pw, i=i, k=k):
                            v.wait_ge(sPE.h, w)
                            v.tensor_copy(hqt[:, (i * 32 + k) * 128:(i * 32 + k + 1) * 128],
                                          p_tr[k % 2][:]).then_inc(sDV.h, 1)
                        sDV.inc()
                hq_ready = sDV.n
                for kp in range(2):
                    for it in range(32):
                        pew = sPE.n
                        wld = sIN.inc(16)
                        @on("sync")
                        def _(s, kp=kp, it=it, pew=pew):
                            s.wait_ge(sPE.h, pew)
                            s.dma_start(out=wdb[it % 2][:],
                                        in_=WDC[it * 128:(it + 1) * 128,
                                                kp * 512:(kp + 1) * 512]).then_inc(sIN.h, 16)
                        pw = sPE.inc()
                        @on("tensor")
                        def _(t, kp=kp, it=it, wld=wld, hqr=hq_ready, dvw=sDV.n):
                            t.wait_ge(sIN.h, wld)
                            if it == 0:
                                t.wait_ge(sDV.h, max(hqr, dvw))
                            for i in range(BLK):
                                t.matmul(p_gu[i][:],
                                         hqt[:, (i * 32 + it) * 128:(i * 32 + it + 1) * 128],
                                         wdb[it % 2][:], start=(it == 0), stop=(it == 31))
                            t.inc_last(sPE.h, 1)
                    pvw = sPE.n
                    odw0 = sOD.n
                    for i in range(BLK):
                        ti = tiles[i]
                        dvw = sDV.n
                        w = sIN.inc(16)
                        @on("sync")
                        def _(s, ti=ti, i=i, dvw=dvw):
                            s.wait_ge(sDV.h, dvw)
                            s.dma_start(out=x0t[i % 2][:],
                                        in_=xsrc[ti * 128:(ti + 1) * 128, :]
                                        ).then_inc(sIN.h, 16)
                        @on("vector")
                        def _(v, w=w, i=i, kp=kp, pvw=pvw, odw=odw0):
                            v.wait_ge(sIN.h, w)
                            v.wait_ge(sPE.h, pvw)
                            if kp == 1:
                                v.wait_ge(sOD.h, odw)
                            v.scalar_tensor_tensor(
                                x1a[:, i * 512:(i + 1) * 512],
                                p_gu[i][:], beth[:, i:i + 1],
                                x0t[i % 2][:, kp * 512:(kp + 1) * 512],
                                ALU.mult, ALU.add).then_inc(sDV.h, 1)
                        sDV.inc()
                    w2 = sDV.n
                    for i in range(BLK):
                        ti = tiles[i]
                        sOD.inc(16)
                        @on("sync")
                        def _(s, ti=ti, i=i, w2=w2, kp=kp):
                            s.wait_ge(sDV.h, w2)
                            s.dma_start(out=xdst[ti * 128:(ti + 1) * 128,
                                                 kp * 512:(kp + 1) * 512],
                                        in_=x1a[:, i * 512:(i + 1) * 512]
                                        ).then_inc(sOD.h, 16)
            barrier()
            S.close()

        ffn_call("f1", XIN, X1D)
        if "x1" in dbg:
            t = dout("x1", [TOK, DM])
            sOD.inc(16)
            @on("sync")
            def _(s, t=t):
                s.dma_start(out=t, in_=X1D[:, :]).then_inc(sOD.h, 16)
        _chk("F1")

        S2 = ExitStack()
        _scopes.append(S2)
        a2b = sbuf(S2, "a2b", [128, DM], BF16)
        a2t = sbuf(S2, "a2t", [128, DM], BF16)
        qlowb = sbuf(S2, "qlowb", [128, QCD], BF16)
        qlt_all = sbuf(S2, "qlt_all", [128, 2 * NT * 128], BF16)
        kt_all = sbuf(S2, "kt_all", [64, NT * 128], BF16)
        vb_all = sbuf(S2, "vb_all", [128, NT * 64], BF16)
        kct_all = sbuf(S2, "kct_all", [64, 64], BF16)
        vct_all = sbuf(S2, "vct_all", [64, 64], BF16)
        k2s = sbuf(S2, "k2s", [128, NT])
        kfb = sbuf(S2, "kfb", [128, HD])
        kscb = sbuf(S2, "kscb", [128, HD])
        knb = sbuf(S2, "knb", [128, HD], BF16)
        k2col = sbuf(S2, "k2col", [128, 1], BF16)
        rt1 = sbuf(S2, "rt1", [128, 16]); rt2 = sbuf(S2, "rt2", [128, 16])
        kcsb = sbuf(S2, "kcsb", [4, 2 * HD], BF16)
        p2_trp = [psum(S2, f"p2_trp{i}", [128, 128], BF16) for i in range(2)]
        p2_ql = psum(S2, "p2_ql", [128, QCD])
        p2_k = psum(S2, "p2_k", [128, HD])
        p2_v = psum(S2, "p2_v", [128, HD])
        p2_kc = psum(S2, "p2_kc", [4, HD])
        p2_kc4 = psum(S2, "p2_kc4", [64, 4], BF16)
        p2_t1 = psum(S2, "p2_t1", [1, 128])

        for ti in range(NT):
            dvw = sDV.n
            w = sIN.inc(16)
            @on("sync")
            def _(s, ti=ti, dvw=dvw):
                s.wait_ge(sDV.h, dvw)
                s.dma_start(out=x0t[0][:], in_=X1D[ti * 128:(ti + 1) * 128, :]
                            ).then_inc(sIN.h, 16)
            aw = sAC.inc()
            @on("scalar")
            def _(a, w=w):
                a.wait_ge(sIN.h, w)
                a.activation(lnsc[:], x0t[0][:], AF.Square, accum_out=v1c[:]
                             ).then_inc(sAC.h, 1)
            @on("vector")
            def _(v, w=w, aw=aw):
                v.wait_ge(sIN.h, w)
                v.reduce_sum(m1c[:], x0t[0][:], axis=AX.X)
                v.tensor_scalar_mul(m1c[:], m1c[:], 1.0 / DM)
                v.wait_ge(sAC.h, aw)
                v.tensor_scalar_mul(v1c[:], v1c[:], 1.0 / DM)
                v.tensor_mul(s1c[:], m1c[:], m1c[:])
                v.tensor_sub(v1c[:], v1c[:], s1c[:])
                v.tensor_scalar_add(v1c[:], v1c[:], 1e-5).then_inc(sDV.h, 1)
            sDV.inc()
            aw = sAC.inc(2)
            @on("scalar")
            def _(a, w=sDV.n):
                a.wait_ge(sDV.h, w)
                a.activation(v1c[:], v1c[:], AF.Ln)
                a.activation(s1c[:], v1c[:], AF.Exp, scale=-0.5).then_inc(sAC.h, 2)
            @on("vector")
            def _(v, aw=aw):
                v.wait_ge(sAC.h, aw)
                v.tensor_scalar(xln[:], x0t[0][:], m1c[:, 0:1], None, ALU.subtract)
                v.tensor_scalar_mul(a2b[:], xln[:], s1c[:, 0:1]).then_inc(sDV.h, 1)
            sDV.inc()
            w2 = sDV.n
            for k in range(8):
                dvw = sDV.n
                pw = sPE.inc()
                @on("tensor")
                def _(t, w2=w2, k=k, dvw=dvw):
                    t.wait_ge(sDV.h, max(w2, dvw - 1))
                    t.transpose(p2_trp[k % 2][:], a2b[:, k * 128:(k + 1) * 128],
                                ident[:]).then_inc(sPE.h, 1)
                @on("vector")
                def _(v, w=pw, k=k):
                    v.wait_ge(sPE.h, w)
                    v.tensor_copy(a2t[:, k * 128:(k + 1) * 128], p2_trp[k % 2][:]
                                  ).then_inc(sDV.h, 1)
                sDV.inc()
            w2 = sDV.n
            pw = sPE.inc()
            @on("tensor")
            def _(t, w2=w2):
                t.wait_ge(sDV.h, w2)
                for k in range(8):
                    t.matmul(p2_ql[:], a2t[:, k * 128:(k + 1) * 128],
                             wqcb[:, k * QCD:(k + 1) * QCD], start=(k == 0), stop=(k == 7))
                for k in range(8):
                    t.matmul(p2_k[:], a2t[:, k * 128:(k + 1) * 128],
                             wkb[:, k * HD:(k + 1) * HD], start=(k == 0), stop=(k == 7))
                for k in range(8):
                    t.matmul(p2_v[:], a2t[:, k * 128:(k + 1) * 128],
                             wvb[:, k * HD:(k + 1) * HD], start=(k == 0), stop=(k == 7))
                t.inc_last(sPE.h, 1)
            @on("vector")
            def _(v, w=pw, ti=ti):
                v.wait_ge(sPE.h, w)
                v.tensor_copy(qlowb[:], p2_ql[:])
                v.tensor_copy(kfb[:], p2_k[:])
                v.tensor_copy(vb_all[:, ti * 64:(ti + 1) * 64], p2_v[:]
                              ).then_inc(sDV.h, 1)
            sDV.inc()
            w2 = sDV.n
            pw = sPE.inc()
            @on("tensor")
            def _(t, w2=w2):
                t.wait_ge(sDV.h, w2)
                t.transpose(p2_trp[0][:], qlowb[:, 0:128], ident[:])
                t.transpose(p2_trp[1][:], qlowb[:, 128:256], ident[:])
                t.inc_last(sPE.h, 1)
            @on("vector")
            def _(v, w=pw, ti=ti):
                v.wait_ge(sPE.h, w)
                v.tensor_copy(qlt_all[:, ti * 128:(ti + 1) * 128], p2_trp[0][:])
                v.tensor_copy(qlt_all[:, (NT + ti) * 128:(NT + ti + 1) * 128],
                              p2_trp[1][:])
                c = rcoss[:, ti * 16:(ti + 1) * 16]; sn = rsins[:, ti * 16:(ti + 1) * 16]
                v.tensor_mul(rt1[:], kfb[:, 0:16], c)
                v.tensor_mul(rt2[:], kfb[:, 16:32], sn)
                v.tensor_sub(kscb[:, 0:16], rt1[:], rt2[:])
                v.tensor_mul(rt1[:], kfb[:, 0:16], sn)
                v.tensor_mul(rt2[:], kfb[:, 16:32], c)
                v.tensor_add(kscb[:, 16:32], rt1[:], rt2[:])
                v.tensor_copy(kscb[:, 32:64], kfb[:, 32:64]).then_inc(sDV.h, 1)
            sDV.inc()
            aw = sAC.inc()
            @on("scalar")
            def _(a, w=sDV.n):
                a.wait_ge(sDV.h, w)
                a.activation(lnsc[:, 0:HD], kscb[:], AF.Square, accum_out=v1c[:]
                             ).then_inc(sAC.h, 1)
            @on("vector")
            def _(v, aw=aw):
                v.wait_ge(sAC.h, aw)
                v.tensor_scalar(v1c[:], v1c[:], 1.0 / HD, 1e-6, ALU.mult, ALU.add
                                ).then_inc(sDV.h, 1)
            sDV.inc()
            aw = sAC.inc(2)
            @on("scalar")
            def _(a, w=sDV.n):
                a.wait_ge(sDV.h, w)
                a.activation(v1c[:], v1c[:], AF.Ln)
                a.activation(s1c[:], v1c[:], AF.Exp, scale=-0.5).then_inc(sAC.h, 2)
            @on("vector")
            def _(v, aw=aw, ti=ti):
                v.wait_ge(sAC.h, aw)
                v.tensor_scalar_mul(kscb[:], kscb[:], s1c[:, 0:1])
                v.tensor_mul(kscb[:], kscb[:], gk128[:])
                v.tensor_scalar_mul(knb[:], kscb[:], 0.125).then_inc(sDV.h, 1)
            sDV.inc()
            aw2 = sAC.inc()
            @on("scalar")
            def _(a, w=sDV.n, ti=ti):
                a.wait_ge(sDV.h, w)
                a.activation(lnsc[:, 0:HD], knb[:], AF.Square,
                             accum_out=k2s[:, ti:ti + 1]).then_inc(sAC.h, 1)
            w2 = sDV.n
            pw = sPE.inc()
            @on("tensor")
            def _(t, w2=w2, ti=ti):
                t.wait_ge(sDV.h, w2)
                t.transpose(p2_trp[0][0:64, :], knb[:], ident[:])
                t.matmul(p2_kc[:], b32avg[:], knb[:], start=True, stop=True)
                t.inc_last(sPE.h, 1)
            @on("vector")
            def _(v, w=pw, ti=ti):
                v.wait_ge(sPE.h, w)
                v.tensor_copy(kt_all[:, ti * 128:(ti + 1) * 128], p2_trp[0][0:64, :])
                v.tensor_copy(kcsb[:, 0:HD], p2_kc[:]).then_inc(sDV.h, 1)
            sDV.inc()
            w2 = sDV.n
            pw = sPE.inc()
            @on("tensor")
            def _(t, w2=w2, ti=ti):
                t.wait_ge(sDV.h, w2)
                t.matmul(p2_kc[:], b32avg[:], vb_all[:, ti * 64:(ti + 1) * 64],
                         start=True, stop=True).then_inc(sPE.h, 1)
            @on("vector")
            def _(v, w=pw):
                v.wait_ge(sPE.h, w)
                v.tensor_copy(kcsb[:, HD:2 * HD], p2_kc[:]).then_inc(sDV.h, 1)
            sDV.inc()
            w2 = sDV.n
            pw = sPE.inc()
            @on("tensor")
            def _(t, w2=w2):
                t.wait_ge(sDV.h, w2)
                t.transpose(p2_kc4[:], kcsb[:, 0:HD], ident[0:4, 0:4]
                            ).then_inc(sPE.h, 1)
            @on("vector")
            def _(v, w=pw, ti=ti):
                v.wait_ge(sPE.h, w)
                v.tensor_copy(kct_all[:, ti * 4:(ti + 1) * 4], p2_kc4[:]
                              ).then_inc(sDV.h, 1)
            sDV.inc()
            w2 = sDV.n
            pw = sPE.inc()
            @on("tensor")
            def _(t, w2=w2):
                t.wait_ge(sDV.h, w2)
                t.transpose(p2_kc4[:], kcsb[:, HD:2 * HD], ident[0:4, 0:4]
                            ).then_inc(sPE.h, 1)
            @on("vector")
            def _(v, w=pw, ti=ti):
                v.wait_ge(sPE.h, w)
                v.tensor_copy(vct_all[:, ti * 4:(ti + 1) * 4], p2_kc4[:]
                              ).then_inc(sDV.h, 1)
            sDV.inc()
        @on("vector")
        def _(v, w=sAC.n):
            v.wait_ge(sAC.h, w)
            v.reduce_max(m1c[:], k2s[:], axis=AX.X)
            v.tensor_copy(k2col[:], m1c[:]).then_inc(sDV.h, 1)
        sDV.inc()
        w2 = sDV.n
        sOD.inc(7 * 16)
        @on("sync")
        def _(s, w2=w2):
            s.wait_ge(sDV.h, w2)
            for hh in range(2):
                s.dma_start(out=CONTRIB[SEC_QL + hh * 2048:SEC_QL + (hh + 1) * 2048,
                                        :].rearrange("(q t) c -> q t c", t=NT),
                            in_=qlt_all[:, hh * 2048:(hh + 1) * 2048].rearrange(
                                "p (t c) -> p t c", c=128)).then_inc(sOD.h, 16)
            s.dma_start(out=CONTRIB[SEC_K:SEC_K + 1024, :].rearrange(
                            "(d t) c -> d t c", t=NT),
                        in_=kt_all[:].rearrange("p (t c) -> p t c", c=128)
                        ).then_inc(sOD.h, 16)
            s.dma_start(out=CONTRIB[SEC_V:SEC_V + 1024, :].rearrange(
                            "(p tt) c -> p tt c", tt=8),
                        in_=vb_all[:].rearrange("p (tt c) -> p tt c", c=128)
                        ).then_inc(sOD.h, 16)
            s.dma_start(out=CONTRIB[SEC_KC:SEC_KC + 64, 0:64], in_=kct_all[:]
                        ).then_inc(sOD.h, 16)
            s.dma_start(out=CONTRIB[SEC_VC:SEC_VC + 64, 0:64], in_=vct_all[:]
                        ).then_inc(sOD.h, 16)
            s.dma_start(out=CONTRIB[SEC_K2:SEC_K2 + 1, :].rearrange("o c -> c o"),
                        in_=k2col[:]).then_inc(sOD.h, 16)
        if "k2s" in dbg:
            t = dout("k2s", [128, NT])
            sOD.inc(16)
            @on("sync")
            def _(s, t=t, w=sDV.n):
                s.wait_ge(sDV.h, w)
                s.dma_start(out=t, in_=k2s[:]).then_inc(sOD.h, 16)
        barrier()
        S2.close()
        _chk("P2")
        gw = sGP.inc()
        @on("gpsimd")
        def _(g, w=sOD.n):
            g.wait_ge(sOD.h, w)
            g.collective_compute("AllGather", ALU.bypass, replica_groups=RG,
                                 ins=[CONTRIB.opt()], outs=[GOUT.opt()]
                                 ).then_inc(sGP.h, 1)

        A = ExitStack()
        _scopes.append(A)
        QN = ExitStack()
        _scopes.append(QN)
        kstack = sbuf(A, "kstack", [98, SEQ], BF16)
        ksw = sbuf(A, "ksw", [98, SEQ], BF16)
        vones = sbuf(A, "vones", [128, NQT * 65], BF16)
        kc98 = sbuf(A, "kc98", [98, 512], BF16)
        vc1 = [sbuf(A, f"vc1_{i}", [128, 65], BF16) for i in range(2)]
        vctg = sbuf(A, "vctg", [64, 256], BF16)
        k2g = sbuf(A, "k2g", [1, 512], BF16)
        qa_h = [sbuf(A, f"qa{h}", [98, SEQ], BF16) for h in range(2)]
        qb_sh = sbuf(A, "qb_sh", [98, SEQ], BF16)
        qab = [[qa_h[0], qb_sh], [qa_h[1], qb_sh]]
        kmb = sbuf(A, "kmb", [128, 1])
        mxts = sbuf(A, "mxts", [128, SEQ], BF16)
        qlt0 = sbuf(QN, "qlt0", [128, SEQ], BF16)
        qlt1 = sbuf(QN, "qlt1", [128, SEQ], BF16)
        mrowb = sbuf(QN, "mrowb", [2, SEQ], BF16)

        G = GOUT.rearrange("(r n) c -> r n c", n=NCB)
        ldw0 = sIN.n
        sIN.inc(23 * 16)
        @on("sync")
        def _(s, w=sGP.n):
            s.wait_ge(sGP.h, w)
            s.dma_start(out=qlt0[:].rearrange("q (r t c) -> q r t c", r=4, c=128),
                        in_=G[:, SEC_QL:SEC_QL + 2048, :].rearrange(
                            "r (q t) c -> q r t c", t=NT)).then_inc(sIN.h, 16)
            s.dma_start(out=qlt1[:].rearrange("q (r t c) -> q r t c", r=4, c=128),
                        in_=G[:, SEC_QL + 2048:SEC_QL + 4096, :].rearrange(
                            "r (q t) c -> q r t c", t=NT)).then_inc(sIN.h, 16)
            kin = G[:, SEC_K:SEC_K + 1024, :].rearrange("r (d t) c -> d r t c", t=NT)
            s.dma_start(out=kstack[0:64, :].rearrange("d (r t c) -> d r t c", r=4, c=128),
                        in_=kin).then_inc(sIN.h, 16)
            s.dma_start(out=ksw[0:64, :].rearrange("d (r t c) -> d r t c", r=4, c=128),
                        in_=kin).then_inc(sIN.h, 16)
            s.dma_start(out=kstack[64:96, :], in_=R32BIGC).then_inc(sIN.h, 16)
            s.dma_start(out=kstack[96:97, :], in_=ONESROWC).then_inc(sIN.h, 16)
            s.dma_start(out=kstack[97:98, :], in_=ZROWC).then_inc(sIN.h, 16)
            s.dma_start(out=ksw[64:96, :], in_=Z32RC).then_inc(sIN.h, 16)
            s.dma_start(out=ksw[96:97, :], in_=ONESROWC).then_inc(sIN.h, 16)
            s.dma_start(out=ksw[97:98, :], in_=ZROWC).then_inc(sIN.h, 16)
            for rr in range(4):
                for t2 in range(2):
                    s.dma_start(
                        out=vones[:].rearrange("p (r tt t2 x) -> p r tt t2 x",
                                               r=4, tt=8, x=65)[:, rr, :, t2, 0:64],
                        in_=G[rr, SEC_V:SEC_V + 1024, :].rearrange(
                            "(p tt) (t2 d) -> p tt t2 d", tt=8, t2=2)[:, :, t2, :]
                        ).then_inc(sIN.h, 16)
            s.dma_start(out=kc98[0:64, 0:NB].rearrange("d (r c) -> d r c", r=4),
                        in_=G[:, SEC_KC:SEC_KC + 64, 0:64].rearrange(
                            "r d c -> d r c")).then_inc(sIN.h, 16)
            s.dma_start(out=kc98[96:97, 0:NB], in_=ONESROWC[:, 0:NB]).then_inc(sIN.h, 16)
            s.dma_start(out=kc98[97:98, 0:NB], in_=ZROWC[:, 0:NB]).then_inc(sIN.h, 16)
            s.dma_start(out=vctg[:].rearrange("d (r c) -> d r c", r=4),
                        in_=G[:, SEC_VC:SEC_VC + 64, 0:64].rearrange(
                            "r d c -> d r c")).then_inc(sIN.h, 16)
            s.dma_start(out=k2g[:].rearrange("o (r c) -> o r c", r=4),
                        in_=G[:, SEC_K2:SEC_K2 + 1, :].rearrange(
                            "r o c -> o r c")).then_inc(sIN.h, 16)
        _pw = sPE.inc(2)
        _dv2 = sDV.inc(2)
        @on("tensor")
        def _(t, w=sIN.n, pw=_pw, dv0=_dv2 - 1):
            t.wait_ge(sIN.h, w)
            t.transpose(pq_vt[:], vctg[:, 0:128], ident[0:64, 0:64]).then_inc(sPE.h, 1)
            t.wait_ge(sDV.h, dv0)
            t.transpose(pq_vt[:], vctg[:, 128:256], ident[0:64, 0:64]).then_inc(sPE.h, 1)
        @on("vector")
        def _(v, pw=_pw):
            v.wait_ge(sPE.h, pw - 1)
            v.tensor_copy(vc1[0][:, 0:64], pq_vt[:])
            v.memset(vc1[0][:, 64:65], 1.0).then_inc(sDV.h, 1)
            v.wait_ge(sPE.h, pw)
            v.tensor_copy(vc1[1][:, 0:64], pq_vt[:])
            v.memset(vc1[1][:, 64:65], 1.0).then_inc(sDV.h, 1)
        @on("vector")
        def _(v, w=sIN.n):
            v.wait_ge(sIN.h, w)
            v.reduce_max(t11[:], k2g[:], axis=AX.X).then_inc(sDV.h, 1)
        sDV.inc()
        aw = sAC.inc(2)
        @on("scalar")
        def _(a, w=sDV.n):
            a.wait_ge(sDV.h, w)
            a.activation(t11[:], t11[:], AF.Ln)
            a.activation(t11[:], t11[:], AF.Exp, scale=0.5).then_inc(sAC.h, 2)
        @on("vector")
        def _(v, aw=aw):
            v.wait_ge(sAC.h, aw)
            v.tensor_scalar_mul(kmaxv[:], t11[:], 1.02)
            v.memset(vones[:].rearrange("p (t x) -> p t x", x=65)[:, :, 64:65], 1.0)
            v.memset(kc98[0:64, NB:512], 0.0)
            v.memset(kc98[64:96, :], 0.0).then_inc(sDV.h, 1)
        sDV.inc()
        kmax_ready = sDV.n

        qf = sbuf(QN, "qf", [128, 512])
        qsq = sbuf(QN, "qsq", [128, 512])
        ss2 = sbuf(QN, "ss2", [128, 8])
        qnb = sbuf(QN, "qnb", [128, 512], BF16)
        rr1 = sbuf(QN, "rr1", [128, 128]); rr2 = sbuf(QN, "rr2", [128, 128])
        pq_qa = psum(QN, "pq_qa", [128, 128])
        pq_qb = psum(QN, "pq_qb", [128, 128])
        pq_ta = psum(QN, "pq_ta", [128, 128], BF16)
        pq_tb = psum(QN, "pq_tb", [128, 128], BF16)
        pq_vt = psum(QN, "pq_vt", [128, 64], BF16)
        pq_k1 = psum(QN, "pq_k1", [128, 1])
        pq_m1 = psum(QN, "pq_m1", [2, 128])

        _pw = sPE.inc()
        @on("tensor")
        def _(t, w=sDV.n):
            t.wait_ge(sDV.h, w)
            t.matmul(pq_k1[:], ones1x128_32[:], kmaxv[:], start=True, stop=True
                     ).then_inc(sPE.h, 1)
        @on("vector")
        def _(v, w=_pw):
            v.wait_ge(sPE.h, w)
            v.tensor_copy(kmb[:], pq_k1[:]).then_inc(sDV.h, 1)
        sDV.inc()
        kmb_ready = sDV.n

        if "k2g" in dbg:
            t = dout("k2g", [1, 512], BF16)
            t2 = dout("kmx", [1, 1])
            sOD.inc(32)
            @on("sync")
            def _(s, t=t, t2=t2, w=kmb_ready):
                s.wait_ge(sDV.h, w)
                s.dma_start(out=t, in_=k2g[:]).then_inc(sOD.h, 16)
                s.dma_start(out=t2, in_=kmaxv[:]).then_inc(sOD.h, 16)
        _chk("PG")
        import os as _osq
        _qn_n = int(_osq.environ.get("K_QN_N", str(NQT // 4)))
        _qn_skip = set(_osq.environ.get("K_QN_SKIP", "").split(","))
        for q4 in range(_qn_n):
            qts = [q4 * 4 + i for i in range(4)]
            dvw = sDV.n
            ev = sDV.inc(4)
            pw = sPE.inc(4)
            @on("tensor")
            def _(t, qts=qts, w=ldw0 + 2 * 16, dvw=dvw, ev=ev):
                t.wait_ge(sIN.h, w)
                if qts[0] > 0:
                    t.wait_ge(sDV.h, dvw)
                for i, qt in enumerate(qts):
                    dst = pq_qa if i % 2 == 0 else pq_qb
                    if i >= 2:
                        t.wait_ge(sDV.h, ev - 3 + (i - 2))
                    t.matmul(dst[:],
                             qlt0[:, qt * 128:(qt + 1) * 128], wqupb[:, 0:128],
                             start=True, stop=False)
                    t.matmul(dst[:],
                             qlt1[:, qt * 128:(qt + 1) * 128], wqupb[:, 128:256],
                             start=False, stop=True)
                    t.inc_last(sPE.h, 1)
            @on("vector")
            def _(v, w=pw, qts=qts):
                for i in range(4):
                    v.wait_ge(sPE.h, w - 3 + i)
                    v.tensor_copy(qf[:, i * 128:(i + 1) * 128],
                                  pq_qa[:] if i % 2 == 0 else pq_qb[:])
                    v.inc_last(sDV.h, 1)
                cv = rcosf[:, qts[0] * 16:(qts[0] + 4) * 16].rearrange(
                    "p (t o x) -> p t o x", o=1, x=16).to_broadcast([128, 4, 2, 16])
                sv = rsinf[:, qts[0] * 16:(qts[0] + 4) * 16].rearrange(
                    "p (t o x) -> p t o x", o=1, x=16).to_broadcast([128, 4, 2, 16])
                q1 = qf[:].rearrange("p (t h x) -> p t h x", h=2, x=64)[:, :, :, 0:16]
                q2 = qf[:].rearrange("p (t h x) -> p t h x", h=2, x=64)[:, :, :, 16:32]
                r1v = rr1[:].rearrange("p (t h x) -> p t h x", h=2, x=16)
                r2v = rr2[:].rearrange("p (t h x) -> p t h x", h=2, x=16)
                v.tensor_mul(r1v, q1, cv)
                v.tensor_mul(r2v, q2, sv)
                v.tensor_sub(r1v, r1v, r2v)
                v.tensor_mul(r2v, q2, cv)
                v.tensor_mul(q2, q1, sv)
                v.tensor_add(q2, q2, r2v)
                v.tensor_copy(q1, r1v).then_inc(sDV.h, 1)
            sDV.inc()
            aw = sAC.inc()
            @on("scalar")
            def _(a, w=sDV.n):
                a.wait_ge(sDV.h, w)
                a.activation(qsq[:], qf[:], AF.Square).then_inc(sAC.h, 1)
            @on("vector")
            def _(v, aw=aw):
                v.wait_ge(sAC.h, aw)
                v.reduce_sum(ss2[:], qsq[:].rearrange("p (g x) -> p g x", x=64), axis=AX.X)
                v.tensor_scalar(ss2[:], ss2[:], 1.0 / HD, 1e-6, ALU.mult, ALU.add
                                ).then_inc(sDV.h, 1)
            sDV.inc()
            aw = sAC.inc(2)
            @on("scalar")
            def _(a, w=sDV.n):
                a.wait_ge(sDV.h, w)
                a.activation(ss2[:], ss2[:], AF.Ln)
                a.activation(ss2[:], ss2[:], AF.Exp, scale=-0.5).then_inc(sAC.h, 2)
            @on("vector")
            def _(v, aw=aw):
                v.wait_ge(sAC.h, aw)
                v.tensor_mul(qf[:], qf[:],
                             ss2[:].rearrange("p (g o) -> p g o", o=1).to_broadcast([128, 8, 64]))
                v.tensor_mul(qnb[:].rearrange("p (g x) -> p g x", x=64),
                             qf[:].rearrange("p (g x) -> p g x", x=64),
                             gq128[:].rearrange("p (o x) -> p o x", o=1).to_broadcast([128, 8, 64])
                             ).then_inc(sDV.h, 1)
            sDV.inc()
            aw = sAC.inc()
            @on("scalar")
            def _(a, w=sDV.n):
                a.wait_ge(sDV.h, w)
                a.activation(qsq[:], qnb[:], AF.Square).then_inc(sAC.h, 1)
            @on("vector")
            def _(v, aw=aw):
                v.wait_ge(sAC.h, aw)
                v.reduce_sum(ss2[:], qsq[:].rearrange("p (g x) -> p g x", x=64), axis=AX.X
                             ).then_inc(sDV.h, 1)
            sDV.inc()
            aw = sAC.inc(2)
            @on("scalar")
            def _(a, w=sDV.n):
                a.wait_ge(sDV.h, w)
                a.activation(ss2[:], ss2[:], AF.Ln)
                a.activation(ss2[:], ss2[:], AF.Exp, scale=0.5).then_inc(sAC.h, 2)
            @on("vector")
            def _(v, aw=aw):
                v.wait_ge(sAC.h, aw)
                v.scalar_tensor_tensor(ss2[:], ss2[:], -1.0,
                                       kmb[:, 0:1].to_broadcast([128, 8]),
                                       ALU.mult, ALU.mult).then_inc(sDV.h, 1)
            sDV.inc()
            w2 = sDV.n
            ev = sDV.inc(4)
            pw = sPE.inc(4)
            @on("tensor")
            def _(t, qts=qts, w2=w2, ev=ev):
                t.wait_ge(sDV.h, w2)
                for i in range(4):
                    dst = pq_ta if i % 2 == 0 else pq_tb
                    if i >= 2:
                        t.wait_ge(sDV.h, ev - 3 + (i - 2))
                    t.transpose(dst[:], qnb[:, i * 128:(i + 1) * 128], ident[:])
                    t.inc_last(sPE.h, 1)
            @on("vector")
            def _(v, qts=qts, w=pw):
                for i in range(4):
                    v.wait_ge(sPE.h, w - 3 + i)
                    src = pq_ta if i % 2 == 0 else pq_tb
                    for hl in range(2):
                        v.tensor_copy(
                            qab[hl][0][0:64, (qts[0] + i) * 128:(qts[0] + i + 1) * 128],
                            src[hl * 64:(hl + 1) * 64, :])
                    v.inc_last(sDV.h, 1)
            if "mt" not in _qn_skip:
                for i, qt in enumerate(qts):
                    w2 = sDV.n
                    pw = sPE.inc()
                    @on("tensor")
                    def _(t, i=i, w2=w2):
                        t.wait_ge(sDV.h, w2)
                        t.transpose(pq_m1[:], ss2[:, i * 2:(i + 1) * 2], ident32[:]
                                    ).then_inc(sPE.h, 1)
                    @on("vector")
                    def _(v, qt=qt, w=pw):
                        v.wait_ge(sPE.h, w)
                        v.tensor_copy(mrowb[0:2, qt * 128:(qt + 1) * 128], pq_m1[:]
                                      ).then_inc(sDV.h, 1)
                    sDV.inc()
        w2 = sDV.n
        sOD.inc(16)
        @on("sync")
        def _(s, w2=w2):
            s.wait_ge(sDV.h, w2)
            s.dma_start(out=MROWD[:, :], in_=mrowb[:]).then_inc(sOD.h, 16)
        odw = sOD.n
        sIN.inc(6 * 16)
        @on("sync")
        def _(s, w=odw):
            s.wait_ge(sOD.h, w)
            for hl in range(2):
                s.dma_start(out=qa_h[hl][96:97, :], in_=MROWD[hl:hl + 1, :]
                            ).then_inc(sIN.h, 16)
                s.dma_start(out=qa_h[hl][97:98, :], in_=ONESROWC).then_inc(sIN.h, 16)
            s.dma_start(out=qb_sh[96:97, :], in_=MROWD[0:1, :]).then_inc(sIN.h, 16)
            s.dma_start(out=qb_sh[97:98, :], in_=ONESROWC).then_inc(sIN.h, 16)
        sIN.inc(16)
        @on("sync")
        def _(s, w=sDV.n):
            s.wait_ge(sDV.h, w)
            s.dma_start(out=qb_sh[0:64, :], in_=qa_h[0][0:64, :]).then_inc(sIN.h, 16)
        q_ready = sIN.n
        barrier()
        if "qq0" in dbg:
            t = dout("qq0", [98, SEQ], BF16)
            sOD.inc(16)
            @on("sync")
            def _(s, t=t):
                s.dma_start(out=t, in_=qab[0][0][:]).then_inc(sOD.h, 16)
        if "qq1" in dbg:
            t = dout("qq1", [98, SEQ], BF16)
            sOD.inc(16)
            @on("sync")
            def _(s, t=t):
                s.dma_start(out=t, in_=qab[1][0][:]).then_inc(sOD.h, 16)
        if "kst" in dbg:
            t = dout("kst", [98, SEQ], BF16)
            sOD.inc(16)
            @on("sync")
            def _(s, t=t):
                s.dma_start(out=t, in_=kstack[:]).then_inc(sOD.h, 16)
        QN.close()
        _chk("QN")

        AT = ExitStack()
        _scopes.append(AT)
        selt = [sbuf(AT, f"selt{i}", [64, 512], BF16) for i in range(4)]
        vaddq4 = sbuf(AT, "vaddq4", [128, 4 * NB], BF16)
        vaddt2 = [sbuf(AT, f"vaddt2_{i}", [128, 512], BF16) for i in range(2)]
        scqm = sbuf(AT, "scqm", [128, NB])
        knk = sbuf(AT, "knk", [128, NB])
        top8 = sbuf(AT, "top8", [128, 16])
        selm = sbuf(AT, "selm", [128, 4 * NB])
        csc = [sbuf(AT, f"csc{i}", [128, 512]) for i in range(2)]
        cexp = [sbuf(AT, f"cexp{i}", [128, 512], BF16) for i in range(2)]
        expb = [sbuf(AT, f"expb{i}", [128, 1024], BF16) for i in range(2)]
        cswsb = sbuf(AT, "cswsb", [128, 256])
        exsw = sbuf(AT, "exsw", [128, 640], BF16)
        esink = sbuf(AT, "esink", [1, 512], BF16)
        compsb = sbuf(AT, "compsb", [65, 512])
        selsb = sbuf(AT, "selsb", [65, 512])
        swsb = sbuf(AT, "swsb", [65, 512])
        g3r = [sbuf(AT, f"g3r{i}", [1, 512]) for i in range(3)]
        g3rb = [sbuf(AT, f"g3rb{i}", [1, 512], BF16) for i in range(3)]
        gnr = sbuf(AT, "gnr", [1, 512])
        t1x = sbuf(AT, "t1x", [1, 512])
        onesr64 = sbuf(AT, "onesr64", [1, 64], BF16)
        pS = [psum(AT, f"pS{i}", [128, 512]) for i in range(4)]
        pP = psum(AT, "pP", [128, 512])
        pT = [psum(AT, f"pT{i}", [128, 128]) for i in range(3)]

        sQA = sem("sQA"); sQE = sem("sQE"); sQD = sem("sQD"); sQP = sem("sQP")

        @on("vector")
        def _(v):
            v.memset(onesr64[:], 1.0).then_inc(sQD.h, 1)
        sQD.inc()

        swT_free = 0
        for hl in range(2):
            if hl == 1:
                sIN.inc(32)
                @on("sync")
                def _(s, aw=sQA.n, plw=sPL.n, pvw=sQP.n):
                    s.wait_ge(sQA.h, aw)
                    s.wait_ge(sQP.h, pvw)
                    s.wait_ge(sPL.h, plw)
                    s.dma_start(out=qb_sh[0:64, :], in_=qa_h[1][0:64, :]
                                ).then_inc(sIN.h, 16)
                    s.dma_start(out=qb_sh[96:97, :], in_=MROWD[1:2, :]).then_inc(sIN.h, 16)
                q_ready = sIN.n
            for qc in range(16):
                qts = [qc * 4 + i for i in range(4)]
                ktmax = 4 * qc + 4
                QA, QB = qab[hl][0], qab[hl][1]
                qsl = slice(qc * 512, (qc + 1) * 512)
                rdv = sQD.n; rqe = sQE.n; rqa = sQA.n; rqp = sQP.n; rpl = sPL.n
                w0 = sIN.inc(48)
                @on("sync")
                def _(s, qts=qts, dvw=rdv):
                    s.wait_ge(sQD.h, dvw)
                    s.dma_start(out=vaddq4[:],
                                in_=VADDQC[:, qts[0] * NB:(qts[0] + 4) * NB]
                                ).then_inc(sIN.h, 16)
                    s.dma_start(out=vaddt2[0][:],
                                in_=VADDTC[0:128, qts[0] * 128:(qts[0] + 4) * 128]
                                ).then_inc(sIN.h, 16)
                    s.dma_start(out=vaddt2[1][:],
                                in_=VADDTC[128:256, qts[0] * 128:(qts[0] + 4) * 128]
                                ).then_inc(sIN.h, 16)
                if qc == 0:
                    @on("vector")
                    def _(v, w=w0):
                        v.wait_ge(sIN.h, w)
                        v.tensor_copy(selm[:], nlt16[:].rearrange("p (o c) -> p o c", o=1
                                      ).to_broadcast([128, 4, NB])).then_inc(sQD.h, 1)
                    sQD.inc()
                else:
                    tk0 = sQD.n
                    for i in range(4):
                        pw = sQA.inc()
                        @on("tensor")
                        def _(t, i=i, qts=qts, w=w0, qr=q_ready, rdv=rdv, rqe=rqe,
                              tk0=tk0, QA=QA):
                            t.wait_ge(sIN.h, max(w, qr))
                            if i < 2:
                                t.wait_ge(sQD.h, rdv)
                                t.wait_ge(sQE.h, rqe)
                            else:
                                t.wait_ge(sQD.h, tk0 + i - 1)
                            t.matmul(pS[2 + i % 2][:],
                                     QA[0:64, qts[i] * 128:(qts[i] + 1) * 128],
                                     kc98[0:64, 0:512], start=True, stop=True
                                     ).then_inc(sQA.h, 1)
                        @on("vector")
                        def _(v, i=i, w=pw):
                            v.wait_ge(sQA.h, w)
                            v.tensor_add(scqm[:], pS[2 + i % 2][:, 0:NB],
                                         vaddq4[:, i * NB:(i + 1) * NB])
                            v.drain()
                            v.max(out=top8[:, 0:8], in_=scqm[:])
                            v.tensor_scalar(knk[:], scqm[:],
                                            top8[:, 7:8], -65536.0, ALU.is_ge, ALU.mult)
                            v.tensor_add(knk[:], knk[:], scqm[:])
                            v.drain()
                            v.max(out=top8[:, 8:16], in_=knk[:])
                            v.tensor_scalar(selm[:, i * NB:(i + 1) * NB],
                                            scqm[:], top8[:, 15:16], 1.0,
                                            ALU.is_ge, ALU.subtract)
                            v.inc_last(sQD.h, 1)
                        sQD.inc()
                w2 = sQD.n
                tb0 = sQD.n
                pw8 = sQA.inc(8)
                @on("tensor")
                def _(t, w2=w2, rqe=rqe, tb0=tb0):
                    t.wait_ge(sQD.h, w2)
                    t.wait_ge(sQE.h, rqe)
                    for j in range(8):
                        if j >= 2:
                            t.wait_ge(sQD.h, tb0 + j - 1)
                        t.transpose(pT[j % 2][:], selm[:, j * 128:(j + 1) * 128],
                                    ident32[:])
                        t.inc_last(sQA.h, 1)
                @on("vector")
                def _(v, pw8=pw8, rpl=rpl):
                    v.wait_ge(sPL.h, rpl)
                    for j in range(8):
                        v.wait_ge(sQA.h, pw8 - 7 + j)
                        i, h = j // 2, j % 2
                        src = pT[j % 2]
                        v.tensor_copy(selt[2 * h][:, i * 128:(i + 1) * 128], src[0:64, :])
                        v.tensor_copy(selt[2 * h + 1][:, i * 128:(i + 1) * 128],
                                      src[64:128, :])
                        v.inc_last(sQD.h, 1)
                sQD.inc(8)
                selt_ready = sQD.n
                bank_free = [("D", tb0 + 7), ("D", tb0 + 8)]
                pwc = sQA.inc()
                @on("tensor")
                def _(t, w=w0, qr=q_ready, rdv=rdv, rqe=rqe, qsl=qsl, QA=QA):
                    t.wait_ge(sIN.h, max(w, qr))
                    t.wait_ge(sQD.h, rdv)
                    t.wait_ge(sQE.h, rqe)
                    t.matmul(pS[0][:], kc98[:, 0:128], QA[:, qsl], start=True, stop=True)
                    t.matmul(pS[1][:], kc98[:, 128:256], QA[:, qsl], start=True, stop=True)
                    t.inc_last(sQA.h, 1)
                aw = sAC.inc()
                @on("scalar")
                def _(a, hl=hl, qsl=qsl, qr=q_ready, rqp=rqp, rqa=rqa, QA=QA):
                    a.wait_ge(sIN.h, qr)
                    a.wait_ge(sQP.h, rqp)
                    a.wait_ge(sQA.h, rqa)
                    a.activation(esink[:], QA[96:97, qsl], AF.Exp,
                                 bias=sinks[0:1, hl:hl + 1]).then_inc(sAC.h, 1)
                esink_ready = sAC.n
                @on("vector")
                def _(v, w=pwc):
                    v.wait_ge(sQA.h, w)
                    v.tensor_add(csc[0][:], pS[0][:], vaddt2[0][:])
                    v.tensor_add(csc[1][:], pS[1][:], vaddt2[1][:])
                    v.inc_last(sQD.h, 1)
                sQD.inc()
                csc_done = sQD.n
                ce = sQE.inc()
                @on("scalar")
                def _(a, w=csc_done, rqp=rqp):
                    a.wait_ge(sQD.h, w)
                    a.wait_ge(sQP.h, rqp)
                    a.activation(cexp[0][:], csc[0][:], AF.Exp)
                    a.activation(cexp[1][:], csc[1][:], AF.Exp)
                    a.inc_last(sQE.h, 1)
                cexp_ready = sQE.n
                pvc = sQP.inc()
                @on("tensor")
                def _(t, w=cexp_ready, ew=esink_ready, rdv=rdv):
                    t.wait_ge(sQE.h, w)
                    t.wait_ge(sAC.h, ew)
                    t.wait_ge(sQD.h, rdv)
                    t.matmul(pP[0:65, :], vc1[0][:], cexp[0][:], start=True, stop=False)
                    t.matmul(pP[0:65, :], vc1[1][:], cexp[1][:], start=False, stop=False)
                    t.matmul(pP[0:65, :], e65[:], esink[:], start=False, stop=True)
                    t.inc_last(sQP.h, 1)
                @on("vector")
                def _(v, w=pvc):
                    v.wait_ge(sQP.h, w)
                    v.tensor_copy(compsb[:], pP[0:65, :]).then_inc(sQD.h, 1)
                sQD.inc()
                comp_evac = sQD.n
                npairs = ktmax // 2
                exp_inc = {}
                pv_inc = {}
                for ch in range((ktmax + 7) // 8):
                    @on("gpsimd")
                    def _(g, ch=ch, sr=selt_ready, aw=sQA.n, QA=QA, QB=QB, qsl=qsl):
                        g.wait_ge(sQD.h, sr)
                        g.wait_ge(sQA.h, aw)
                        dst = (QA if ch % 2 == 0 else QB)
                        g.tensor_copy(dst[64:96, qsl],
                                      selt[ch // 2][(ch % 2) * 32:(ch % 2) * 32 + 32, :]
                                      ).then_inc(sPL.h, 1)
                    sPL.inc()
                    pl_ch = sPL.n
                    for g_ in range(ch * 4, min(npairs, ch * 4 + 4)):
                        kts = [2 * g_, 2 * g_ + 1]
                        par = g_ % 2
                        pw = sQA.inc()
                        @on("tensor")
                        def _(t, g_=g_, kts=kts, par=par, pl=pl_ch, ch=ch,
                              QA=QA, QB=QB, qsl=qsl, cd=csc_done, rqe=rqe,
                              ei=exp_inc.get(g_ - 2, 0)):
                            t.wait_ge(sPL.h, pl)
                            if g_ < 2:
                                t.wait_ge(sQD.h, cd)
                                t.wait_ge(sQE.h, rqe)
                            else:
                                t.wait_ge(sQE.h, ei)
                            src = (QA if ch % 2 == 0 else QB)
                            for j, kt in enumerate(kts):
                                t.matmul(pS[2 * par + j][:],
                                         kstack[:, kt * 128:(kt + 1) * 128],
                                         src[:, qsl], start=True, stop=True)
                            t.inc_last(sQA.h, 1)
                        if qc == 0:
                            @on("vector")
                            def _(v, g_=g_, kts=kts, par=par, w=pw, ce=cexp_ready,
                                  pe=exp_inc.get(g_ - 1, 0)):
                                v.wait_ge(sQA.h, w)
                                v.wait_ge(sQE.h, max(ce, pe))
                                for j, kt in enumerate(kts):
                                    for i in range(4):
                                        dst = csc[j][:, i * 128:(i + 1) * 128]
                                        srcp = pS[2 * par + j][:, i * 128:(i + 1) * 128]
                                        if kt == i:
                                            v.tensor_add(dst, srcp, causadd[:])
                                        elif kt > i:
                                            v.tensor_add(dst, srcp, fullneg[:])
                                        else:
                                            v.tensor_copy(dst, srcp)
                                v.inc_last(sQD.h, 1)
                            sQD.inc()
                            mw = sQD.n
                            sQE.inc()
                            @on("scalar")
                            def _(a, par=par, w=mw):
                                a.wait_ge(sQD.h, w)
                                a.activation(expb[par][:, 0:512], csc[0][:], AF.Exp)
                                a.activation(expb[par][:, 512:1024], csc[1][:], AF.Exp)
                                a.inc_last(sQE.h, 1)
                        else:
                            sQE.inc()
                            @on("scalar")
                            def _(a, par=par, w=pw, pvw=pv_inc.get(g_ - 2, 0)):
                                a.wait_ge(sQA.h, w)
                                if pvw:
                                    a.wait_ge(sQP.h, pvw)
                                a.activation(expb[par][:, 0:512], pS[2 * par][:], AF.Exp)
                                a.activation(expb[par][:, 512:1024], pS[2 * par + 1][:],
                                             AF.Exp)
                                a.inc_last(sQE.h, 1)
                        exp_inc[g_] = sQE.n
                        pv = sQP.inc()
                        @on("tensor")
                        def _(t, g_=g_, kts=kts, par=par, w=exp_inc[g_],
                              ce=comp_evac, npairs=npairs):
                            t.wait_ge(sQE.h, w)
                            if g_ == 0:
                                t.wait_ge(sQD.h, ce)
                            for j, kt in enumerate(kts):
                                t.matmul(pP[0:65, :], vones[:, kt * 65:(kt + 1) * 65],
                                         expb[par][:, j * 512:(j + 1) * 512],
                                         start=(g_ == 0 and j == 0),
                                         stop=(g_ == npairs - 1 and j == 1))
                            t.inc_last(sQP.h, 1)
                        pv_inc[g_] = sQP.n
                sel_done = sQP.n
                @on("vector")
                def _(v, w=sel_done):
                    v.wait_ge(sQP.h, w)
                    v.tensor_copy(selsb[:], pP[0:65, :]).then_inc(sQD.h, 1)
                sQD.inc()
                sel_evac = sQD.n
                gate_exp = [0, 0, 0]
                csw_free = [rqe, rqe]
                exsw_free = rqp
                sw_exp_last = 0
                for i, qt in enumerate(qts):
                    if i < 3:
                        b = i
                        pwg = sQA.inc()
                        @on("tensor")
                        def _(t, b=b, qsl=qsl, QA=QA, se=exp_inc[npairs - 1],
                              ge=gate_exp[b - 1] if b else 0):
                            t.wait_ge(sQE.h, max(se, ge))
                            t.matmul(pS[0][0:1, :], wg98[:, b:b + 1], QA[:, qsl],
                                     start=True, stop=True).then_inc(sQA.h, 1)
                        sQE.inc()
                        @on("scalar")
                        def _(a, b=b, w=pwg, rdv=rdv):
                            a.wait_ge(sQA.h, w)
                            a.wait_ge(sQD.h, rdv)
                            a.activation(g3r[b][:], pS[0][0:1, :], AF.Exp, scale=-1.0
                                         ).then_inc(sQE.h, 1)
                        gate_exp[b] = sQE.n
                    kts = [k for k in range(qt - 4, qt + 1) if k >= 0]
                    nk = len(kts)
                    for j, kt in enumerate(kts):
                        pw = sQA.inc()
                        @on("tensor")
                        def _(t, j=j, kt=kt, qt=qt, bf=bank_free[j % 2], QA=QA):
                            t.wait_ge(sQD.h if bf[0] == "D" else sQE.h, bf[1])
                            t.matmul(pT[j % 2][:], ksw[:, kt * 128:(kt + 1) * 128],
                                     QA[:, qt * 128:(qt + 1) * 128],
                                     start=True, stop=True).then_inc(sQA.h, 1)
                        if kt == qt or kt == qt - 4:
                            slot = 1 if kt == qt else 0
                            @on("vector")
                            def _(v, j=j, kt=kt, qt=qt, slot=slot, w=pw,
                                  cf=csw_free[slot]):
                                v.wait_ge(sQA.h, w)
                                v.wait_ge(sQE.h, cf)
                                v.tensor_add(cswsb[:, slot * 128:(slot + 1) * 128],
                                             pT[j % 2][:],
                                             causadd[:] if kt == qt else wineadd[:])
                                v.inc_last(sQD.h, 1)
                            sQD.inc()
                            mw = sQD.n
                            sQE.inc()
                            @on("scalar")
                            def _(a, j=j, slot=slot, w=mw, xf=exsw_free):
                                a.wait_ge(sQD.h, w)
                                if xf:
                                    a.wait_ge(sQP.h, xf)
                                a.activation(exsw[:, j * 128:(j + 1) * 128],
                                             cswsb[:, slot * 128:(slot + 1) * 128],
                                             AF.Exp).then_inc(sQE.h, 1)
                            csw_free[slot] = sQE.n
                        else:
                            sQE.inc()
                            @on("scalar")
                            def _(a, j=j, w=pw, xf=exsw_free):
                                a.wait_ge(sQA.h, w)
                                if xf:
                                    a.wait_ge(sQP.h, xf)
                                a.activation(exsw[:, j * 128:(j + 1) * 128],
                                             pT[j % 2][:], AF.Exp).then_inc(sQE.h, 1)
                        bank_free[j % 2] = ("E", sQE.n)
                        sw_exp_last = sQE.n
                    pv = sQP.inc()
                    @on("tensor")
                    def _(t, i=i, kts=kts, nk=nk, w=sw_exp_last, es=esink_ready,
                          evw=swT_free):
                        t.wait_ge(sQE.h, w)
                        t.wait_ge(sAC.h, es)
                        if evw:
                            t.wait_ge(sQD.h, evw)
                        t.matmul(pT[2][0:65, :], e65[:],
                                 esink[0:1, i * 128:(i + 1) * 128],
                                 start=True, stop=False)
                        for j, kt in enumerate(kts):
                            t.matmul(pT[2][0:65, :], vones[:, kt * 65:(kt + 1) * 65],
                                     exsw[:, j * 128:(j + 1) * 128],
                                     start=False, stop=(j == nk - 1))
                        t.inc_last(sQP.h, 1)
                    exsw_free = sQP.n
                    @on("vector")
                    def _(v, i=i, w=pv):
                        v.wait_ge(sQP.h, w)
                        v.tensor_copy(swsb[:, i * 128:(i + 1) * 128], pT[2][0:65, :]
                                      ).then_inc(sQD.h, 1)
                    sQD.inc()
                    swT_free = sQD.n
                @on("vector")
                def _(v, qc=qc, wg=gate_exp[2]):
                    v.wait_ge(sQE.h, wg)
                    for b in range(3):
                        v.tensor_scalar_add(g3r[b][:], g3r[b][:], 1.0)
                        v.reciprocal(g3r[b][:], g3r[b][:])
                    v.tensor_add(gnr[:], g3r[0][:], g3r[1][:])
                    v.tensor_add(gnr[:], gnr[:], g3r[2][:])
                    v.tensor_scalar_max(gnr[:], gnr[:], 1e-6)
                    v.reciprocal(gnr[:], gnr[:])
                    v.reciprocal(t1x[:], compsb[64:65, :])
                    v.tensor_mul(g3r[0][:], g3r[0][:], t1x[:])
                    v.tensor_mul(g3r[0][:], g3r[0][:], gnr[:])
                    if qc == 0:
                        v.tensor_mul(g3r[0][0:1, 0:128], g3r[0][0:1, 0:128], z31r[:])
                    v.reciprocal(t1x[:], selsb[64:65, :])
                    v.tensor_mul(g3r[1][:], g3r[1][:], t1x[:])
                    v.tensor_mul(g3r[1][:], g3r[1][:], gnr[:])
                    v.reciprocal(t1x[:], swsb[64:65, :])
                    v.tensor_mul(g3r[2][:], g3r[2][:], t1x[:])
                    v.tensor_mul(g3r[2][:], g3r[2][:], gnr[:])
                    for b in range(3):
                        v.tensor_copy(g3rb[b][:], g3r[b][:])
                    v.inc_last(sQD.h, 1)
                sQD.inc()
                fac_done = sQD.n
                pwb = sQA.inc()
                @on("tensor")
                def _(t, w=fac_done):
                    t.wait_ge(sQD.h, w)
                    t.matmul(pS[1][0:64, :], onesr64[:], g3rb[0][:], start=True, stop=True)
                    t.matmul(pS[2][0:64, :], onesr64[:], g3rb[1][:], start=True, stop=True)
                    t.matmul(pS[3][0:64, :], onesr64[:], g3rb[2][:], start=True, stop=True)
                    t.inc_last(sQA.h, 1)
                @on("vector")
                def _(v, hl=hl, qsl=qsl, w=pwb):
                    v.wait_ge(sQA.h, w)
                    v.tensor_mul(csc[0][0:64, :], compsb[0:64, :], pS[1][0:64, :])
                    v.tensor_mul(csc[1][0:64, :], selsb[0:64, :], pS[2][0:64, :])
                    v.tensor_add(csc[0][0:64, :], csc[0][0:64, :], csc[1][0:64, :])
                    v.tensor_mul(csc[1][0:64, :], swsb[0:64, :], pS[3][0:64, :])
                    v.tensor_add(mxts[hl * 64:(hl + 1) * 64, qsl],
                                 csc[0][0:64, :], csc[1][0:64, :])
                    v.inc_last(sQD.h, 1)
                sQD.inc()
        barrier()
        AT.close()
        if "mx" in dbg:
            t = dout("mx", [128, SEQ], BF16)
            sOD.inc(16)
            @on("sync")
            def _(s, t=t):
                s.dma_start(out=t, in_=mxts[:]).then_inc(sOD.h, 16)
        _chk("ATT")

        sOD.inc(16)
        @on("sync")
        def _(s):
            s.dma_start(out=G2C.rearrange("(r h j) c -> h r j c", r=4, j=16),
                        in_=mxts[:].rearrange("h (r j c) -> h r j c", r=4, c=128)
                        ).then_inc(sOD.h, 16)
        gw = sGP.inc()
        @on("gpsimd")
        def _(g, w=sOD.n):
            g.wait_ge(sOD.h, w)
            g.collective_compute("AllGather", ALU.bypass, replica_groups=RG,
                                 ins=[G2C.opt()], outs=[G2OUT.opt()]).then_inc(sGP.h, 1)
        OP = ExitStack()
        _scopes.append(OP)
        mxt_p = [sbuf(OP, f"mxtp{p}", [128, TOK], BF16) for p in range(4)]
        woutb = sbuf(OP, "woutb", [128, 8 * DM], BF16)
        sGP.inc(16)
        @on("gpsimd")
        def _(g):
            g.dma_start(out=woutb[:].rearrange("p (a q) -> p a q", q=DM),
                        in_=WOUT.rearrange("(a p) q -> p a q", p=128)
                        ).then_inc(sGP.h, 16)
        gw2 = sGP.n
        for p in range(4):
            sGP.inc(16)
            @on("gpsimd")
            def _(g, p=p, w=gw2):
                g.wait_ge(sGP.h, w)
                g.indirect_dma_start(
                    out=mxt_p[p][:], out_offset=None,
                    in_=G2OUT.rearrange("(a j) c -> a (j c)", j=16),
                    in_offset=IndirectOffsetOnAxis(ap=g2idx[:, p:p + 1], axis=0),
                ).then_inc(sGP.h, 16)
        barrier()

        y1b = sbuf(OP, "y1b", [128, DM], BF16)
        y1t = sbuf(OP, "y1t", [128, DM], BF16)
        x2t = sbuf(OP, "x2t", [128, DM])
        po_y1 = [psum(OP, f"po_y1_{h}", [128, 512]) for h in range(2)]
        po_t = [psum(OP, f"po_t{i}", [128, 128], BF16) for i in range(2)]
        po_y2 = [psum(OP, f"po_y2_{h}", [128, 512]) for h in range(2)]

        for ti in range(NT):
            dvw = sDV.n
            pw = sPE.inc()
            @on("tensor")
            def _(t, ti=ti, w=sGP.n, dvw=dvw):
                t.wait_ge(sGP.h, w)
                if ti > 0:
                    t.wait_ge(sDV.h, dvw)
                for gi in range(2):
                    for pp in range(2):
                        p = gi * 2 + pp
                        t.matmul(po_y1[gi][:],
                                 mxt_p[p][:, ti * 128:(ti + 1) * 128],
                                 wogb[:, p * 512:(p + 1) * 512],
                                 start=(pp == 0), stop=(pp == 1))
                t.inc_last(sPE.h, 1)
            @on("vector")
            def _(v, w=pw):
                v.wait_ge(sPE.h, w)
                v.tensor_copy(y1b[:, 0:512], po_y1[0][:])
                v.tensor_copy(y1b[:, 512:1024], po_y1[1][:]).then_inc(sDV.h, 1)
            sDV.inc()
            w2 = sDV.n
            ev = sDV.inc(8)
            pw = sPE.inc(8)
            @on("tensor")
            def _(t, w2=w2, ev=ev):
                t.wait_ge(sDV.h, w2)
                for k in range(8):
                    if k >= 2:
                        t.wait_ge(sDV.h, ev - 7 + (k - 2))
                    t.transpose(po_t[k % 2][:], y1b[:, k * 128:(k + 1) * 128], ident[:])
                    t.inc_last(sPE.h, 1)
            @on("vector")
            def _(v, w=pw):
                for k in range(8):
                    v.wait_ge(sPE.h, w - 7 + k)
                    v.tensor_copy(y1t[:, k * 128:(k + 1) * 128], po_t[k % 2][:])
                    v.inc_last(sDV.h, 1)
            w2 = sDV.n
            pw = sPE.inc()
            @on("tensor")
            def _(t, w2=w2):
                t.wait_ge(sDV.h, w2)
                for half in range(2):
                    for k in range(8):
                        t.matmul(po_y2[half][:], y1t[:, k * 128:(k + 1) * 128],
                                 woutb[:, k * DM + half * 512:k * DM + (half + 1) * 512],
                                 start=(k == 0), stop=(k == 7))
                t.inc_last(sPE.h, 1)
            dvw = sDV.n
            w = sIN.inc(16)
            @on("sync")
            def _(s, ti=ti, dvw=dvw):
                s.wait_ge(sDV.h, dvw)
                s.dma_start(out=x0t[1][:], in_=X1D[ti * 128:(ti + 1) * 128, :]
                            ).then_inc(sIN.h, 16)
            @on("vector")
            def _(v, w=w, pw=pw):
                v.wait_ge(sIN.h, w)
                v.wait_ge(sPE.h, pw)
                v.tensor_add(x2t[:, 0:512], po_y2[0][:], x0t[1][:, 0:512])
                v.tensor_add(x2t[:, 512:1024], po_y2[1][:], x0t[1][:, 512:1024]
                             ).then_inc(sDV.h, 1)
            sDV.inc()
            w2 = sDV.n
            sOD.inc(16)
            @on("sync")
            def _(s, ti=ti, w2=w2):
                s.wait_ge(sDV.h, w2)
                s.dma_start(out=X2D[ti * 128:(ti + 1) * 128, :], in_=x2t[:]
                            ).then_inc(sOD.h, 16)
        barrier()
        OP.close()
        A.close()
        _chk("OP")

        if "x2" in dbg:
            t = dout("x2", [TOK, DM])
            sOD.inc(16)
            @on("sync")
            def _(s, t=t):
                s.dma_start(out=t, in_=X2D[:, :]).then_inc(sOD.h, 16)

        ffn_call("f2", X2D, OUT)
    try:
        _main()
    except _Done:
        pass


    @on("sync")
    def _(s, w=sOD.n):
        s.wait_ge(sOD.h, w)

    for _e in ENGS:
        @on(_e)
        def _(eng):
            eng.nop()
    fsems = {e: root.enter_context(nc.semaphore(f"fence_{e}")) for e in ENGS}
    with root, P:
        with nc.Block() as block:
            @block.sync
            def _(eng):
                eng = EngTrack(eng, fsems["sync"])
                for fn in plan["sync"]:
                    fn(eng)
            @block.tensor
            def _(eng):
                eng = EngTrack(eng, fsems["tensor"])
                for fn in plan["tensor"]:
                    fn(eng)
            @block.vector
            def _(eng):
                eng = EngTrack(eng, fsems["vector"])
                for fn in plan["vector"]:
                    fn(eng)
            @block.scalar
            def _(eng):
                eng = EngTrack(eng, fsems["scalar"])
                for fn in plan["scalar"]:
                    fn(eng)
            @block.gpsimd
            def _(eng):
                eng = EngTrack(eng, fsems["gpsimd"])
                for fn in plan["gpsimd"]:
                    fn(eng)
    return nc, dbg_t


_CACHE = {}


def make_in_maps(inputs, cosf, sinf, consts):
    x = np.ascontiguousarray(np.asarray(inputs["L0_attn_h"], np.float32)
                             ).reshape(2 * SEQ, DM)
    wqup = np.asarray(inputs["L0_attn_w_qup"], np.float32)
    sinkl = np.asarray(inputs["L0_attn_sink_logit"], np.float32).reshape(1, 8)
    base = dict(
        WQC=np.ascontiguousarray(np.asarray(inputs["L0_attn_w_qc"], np.float32)),
        WK=np.ascontiguousarray(np.asarray(inputs["L0_attn_w_k"], np.float32)),
        WV=np.ascontiguousarray(np.asarray(inputs["L0_attn_w_v"], np.float32)),
        GQN=np.ascontiguousarray(np.asarray(inputs["L0_attn_g_qnorm"], np.float32
                                            ).reshape(1, HD)),
        GKN=np.ascontiguousarray(np.asarray(inputs["L0_attn_g_knorm"], np.float32
                                            ).reshape(1, HD)),
        WGATE=np.ascontiguousarray(np.asarray(inputs["L0_attn_w_gate"], np.float32)),
        BGATE=np.ascontiguousarray(np.asarray(inputs["L0_attn_b_gate"], np.float32
                                              ).reshape(1, 3)),
        WOG=np.ascontiguousarray(np.asarray(inputs["L0_attn_w_out_group"], np.float32)),
        WOUT=np.ascontiguousarray(np.asarray(inputs["L0_attn_w_out"], np.float32)),
        WGT=np.ascontiguousarray(np.asarray(inputs["L0_ffn_w_gate"], np.float32).T),
        WUT=np.ascontiguousarray(np.asarray(inputs["L0_ffn_w_up"], np.float32).T),
        WDT=np.ascontiguousarray(np.asarray(inputs["L0_ffn_w_down"], np.float32).T),
    )
    base.update(consts)
    in_maps = []
    for c in range(8):
        hp = c % 4
        j0 = hp * 16
        t0 = j0 * 128
        half = 16
        cs = cosf[t0:t0 + TOK].reshape(NT, 128, half).transpose(1, 0, 2
                                                                ).reshape(128, NT * half)
        sn = sinf[t0:t0 + TOK].reshape(NT, 128, half).transpose(1, 0, 2
                                                                ).reshape(128, NT * half)
        idx = np.empty((128, 4), np.int32)
        for p in range(4):
            idx[:, p] = p * 512 + (j0 // 16) * 128 + np.arange(128)
        m = dict(base)
        m["XIN"] = np.ascontiguousarray(x[c * TOK:(c + 1) * TOK])
        m["WQUPM"] = np.ascontiguousarray(wqup[:, hp * 128:(hp + 1) * 128])
        m["MYSINK"] = np.ascontiguousarray(sinkl[:, 2 * hp:2 * hp + 2])
        m["RCOSS"] = np.ascontiguousarray(cs)
        m["RSINS"] = np.ascontiguousarray(sn)
        m["G2IDX"] = idx
        in_maps.append(m)
    return in_maps


def kernel(**inputs):
    consts, cosf, sinf = host_consts()
    if "prog" not in _CACHE:
        _CACHE["prog"] = build_program(DEBUG_OUTS)
    nc, dbg_t = _CACHE["prog"]
    in_maps = make_in_maps(inputs, cosf, sinf, consts)
    res = run_bass_kernel_spmd(nc, in_maps, core_ids=list(range(8)))
    kernel._last_results = res
    out = np.concatenate([np.asarray(res.results[c]["OUT"]) for c in range(8)], axis=0)
    return out.reshape(2, SEQ, DM)


# version 04 (final)
# speedup vs baseline: 1.0100x; 1.0100x over previous
import numpy as np
import ml_dtypes
from contextlib import ExitStack

import concourse.bass as bass
import concourse.mybir as mybir
from concourse.bass import IndirectOffsetOnAxis
from concourse.bass_utils import run_bass_kernel_spmd

F32 = mybir.dt.float32
BF16 = mybir.dt.bfloat16
I32 = mybir.dt.int32
AF = mybir.ActivationFunctionType
ALU = mybir.AluOpType
AX = mybir.AxisListType

MAGIC = 12582912.0
BIG = 32768.0
TOK, NT, SEQ, NQT = 2048, 16, 8192, 64
DM, I4, QCD, HD, NB = 1024, 4096, 256, 64, 256
BLK = 4

SEC_QL, SEC_K, SEC_V, SEC_KC, SEC_VC, SEC_K2 = 0, 4096, 5120, 6144, 6208, 6272
NCB = 6336
G2R = 8192
RG = [[0, 1, 2, 3], [4, 5, 6, 7]]

DEBUG_OUTS = []
TRUNC = None


class _Done(Exception):
    pass


def host_consts():
    c = {}
    c["IDENT"] = np.eye(128, dtype=ml_dtypes.bfloat16)
    c["IDENT32"] = np.eye(128, dtype=np.float32)
    c["ONES1x128_32"] = np.ones((1, 128), np.float32)
    k = np.arange(SEQ)
    c["R32BIG"] = (BIG * ((k[None, :] // 32) % 32 == np.arange(32)[:, None])
                   ).astype(ml_dtypes.bfloat16)
    c["ONESROW"] = np.ones((1, SEQ), ml_dtypes.bfloat16)
    c["ZROW"] = np.zeros((1, SEQ), ml_dtypes.bfloat16)
    c["Z32R"] = np.zeros((32, SEQ), ml_dtypes.bfloat16)
    e65 = np.zeros((1, 65), ml_dtypes.bfloat16); e65[0, 64] = 1.0
    c["E65"] = e65
    r = np.arange(128)
    c["CAUSAL_ADD"] = np.where(r[None, :] >= r[:, None], 0.0, -BIG
                               ).astype(ml_dtypes.bfloat16)
    c["WINEDGE_ADD"] = np.where(r[None, :] < r[:, None], 0.0, -BIG
                                ).astype(ml_dtypes.bfloat16)
    c["FULLNEG"] = np.full((128, 128), -BIG, ml_dtypes.bfloat16)
    c["B32AVG"] = ((r[:, None] // 32 == np.arange(4)[None, :]) / 32.0
                   ).astype(ml_dtypes.bfloat16)
    n = np.arange(NB)
    q = np.arange(SEQ)
    c["VADDT"] = np.where((32 * n[:, None] + 31) > q[None, :], -BIG, 0.0
                          ).astype(ml_dtypes.bfloat16)
    vq = np.empty((128, NQT * NB), ml_dtypes.bfloat16)
    for qt in range(NQT):
        qv = qt * 128 + r
        vq[:, qt * NB:(qt + 1) * NB] = np.where(
            (32 * n[None, :] + 31) > qv[:, None], -BIG, 0.0)
    c["VADDQ"] = vq
    c["NLT16M1"] = np.broadcast_to((n < 16).astype(np.float32) - 1.0,
                                   (128, NB)).astype(ml_dtypes.bfloat16).copy()
    z31r = np.ones((1, 128), np.float32); z31r[0, :31] = 0.0
    c["Z31R"] = z31r
    half = 16
    inv = 1.0 / (10000.0 ** (np.arange(half, dtype=np.float64) * 2.0 / 32.0))
    ang = np.arange(SEQ, dtype=np.float64)[:, None] * inv[None, :]
    cosf = np.cos(ang).astype(np.float32)
    sinf = np.sin(ang).astype(np.float32)
    c["RCOSF"] = np.ascontiguousarray(
        cosf.reshape(NQT, 128, half).transpose(1, 0, 2).reshape(128, NQT * half))
    c["RSINF"] = np.ascontiguousarray(
        sinf.reshape(NQT, 128, half).transpose(1, 0, 2).reshape(128, NQT * half))
    return c, cosf, sinf


def build_program(debug=()):
    nc = bass.Bass("TRN2", target_bir_lowering=False)
    dbg = set(debug)

    def din(name, shape, dt=F32):
        return nc.dram_tensor(name, shape, dt, kind="ExternalInput").ap()

    XIN = din("XIN", [TOK, DM])
    WQC = din("WQC", [DM, QCD])
    WQUPM = din("WQUPM", [QCD, 128])
    WK = din("WK", [DM, HD]); WV = din("WV", [DM, HD])
    GQN = din("GQN", [1, HD]); GKN = din("GKN", [1, HD])
    WGATE = din("WGATE", [HD, 3]); BGATE = din("BGATE", [1, 3])
    WOG = din("WOG", [2, QCD, 512])
    WOUT = din("WOUT", [DM, DM])
    WGT = din("WGT", [DM, I4]); WUT = din("WUT", [DM, I4])
    WDT = din("WDT", [I4, DM])
    MYSINK = din("MYSINK", [1, 2])
    IDENT = din("IDENT", [128, 128], BF16)
    IDENT32 = din("IDENT32", [128, 128], F32)
    ONES1x128_32C = din("ONES1x128_32", [1, 128], F32)
    R32BIGC = din("R32BIG", [32, SEQ], BF16)
    ONESROWC = din("ONESROW", [1, SEQ], BF16)
    ZROWC = din("ZROW", [1, SEQ], BF16)
    Z32RC = din("Z32R", [32, SEQ], BF16)
    E65C = din("E65", [1, 65], BF16)
    CAUSC = din("CAUSAL_ADD", [128, 128], BF16)
    WINEC = din("WINEDGE_ADD", [128, 128], BF16)
    FULLNEGC = din("FULLNEG", [128, 128], BF16)
    B32C = din("B32AVG", [128, 4], BF16)
    VADDTC = din("VADDT", [NB, SEQ], BF16)
    VADDQC = din("VADDQ", [128, NQT * NB], BF16)
    NLT16C = din("NLT16M1", [128, NB], BF16)
    Z31RC = din("Z31R", [1, 128], F32)
    RCOSF = din("RCOSF", [128, NQT * 16]); RSINF = din("RSINF", [128, NQT * 16])
    RCOSS = din("RCOSS", [128, NT * 16]); RSINS = din("RSINS", [128, NT * 16])
    G2IDXC = din("G2IDX", [128, 4], I32)

    OUT = nc.dram_tensor("OUT", [TOK, DM], F32, kind="ExternalOutput").ap()
    dbg_t = {}
    def dout(name, shape, dt=F32):
        dbg_t[name] = nc.dram_tensor("DBG_" + name, shape, dt,
                                     kind="ExternalOutput").ap()
        return dbg_t[name]

    WGC = nc.dram_tensor("WGC", [DM, I4], BF16).ap()
    WUC = nc.dram_tensor("WUC", [DM, I4], BF16).ap()
    WDC = nc.dram_tensor("WDC", [I4, DM], BF16).ap()
    X1D = nc.dram_tensor("X1D", [TOK, DM], F32).ap()
    X2D = nc.dram_tensor("X2D", [TOK, DM], F32).ap()
    CONTRIB = nc.dram_tensor("CONTRIB", [NCB, 128], BF16).ap()
    GOUT = nc.dram_tensor("GOUT", [4 * NCB, 128], BF16).ap()
    G2C = nc.dram_tensor("G2C", [G2R, 128], BF16).ap()
    G2OUT = nc.dram_tensor("G2OUT", [4 * G2R, 128], BF16).ap()
    MROWD = nc.dram_tensor("MROWD", [2, SEQ], BF16).ap()

    ENGS = ("sync", "tensor", "vector", "scalar", "gpsimd")
    plan = {e: [] for e in ENGS}

    class EngTrack:
        _SKIP = {"wait_ge", "nop", "maybe_wait_ge", "wait_op", "drain"}
        _NOSMALL = {"dma_start", "collective_compute", "indirect_dma_start",
                    "matmul", "transpose", "ldweights", "memset", "memzero"}
        _FENCE = {"reduce_sum", "reduce_max", "tensor_reduce", "max",
                  "reciprocal", "bn_stats", "bn_aggr"}

        def __init__(self, eng, fence_sem):
            self._e = eng
            self._last = None
            self._fs = fence_sem
            self._fn = 0

        def fence(self):
            self._e.drain()

        def __getattr__(self, name):
            f = getattr(self._e, name)
            if callable(f) and not name.startswith("_"):
                track = name not in EngTrack._SKIP
                do_fence = name in EngTrack._FENCE

                def wrap(*a, _f=f, _track=track, _df=do_fence, _n=name, **k):
                    r = _f(*a, **k)
                    if r is not None and hasattr(r, "then_inc"):
                        self._last = r
                        if _track:
                            df = _df
                            if not df and _n not in EngTrack._NOSMALL:
                                out = a[0] if a else k.get("out")
                                shp = getattr(out, "shape", None)
                                if shp is not None and len(shp) >= 2:
                                    fe = 1
                                    for d_ in shp[1:]:
                                        fe *= d_
                                    df = fe <= 16
                            if df:
                                self.fence()
                    return r
                return wrap
            return f

        def inc_last(self, sem, n=1):
            self._last.then_inc(sem, n)

    def on(e):
        def deco(fn):
            plan[e].append(fn)
        return deco

    class Ctr:
        __slots__ = ("h", "n")
        def __init__(self, h): self.h = h; self.n = 0
        def inc(self, k=1): self.n += k; return self.n

    root = ExitStack()
    def sem(name):
        return Ctr(root.enter_context(nc.semaphore(name)))

    def sbuf(scope, name, shape, dt=F32):
        return scope.enter_context(nc.sbuf_tensor(name, shape, dt))

    def psum(scope, name, shape, dt=F32):
        return scope.enter_context(nc.psum_tensor(name, shape, dt))

    sIN = sem("sIN"); sSC = sem("sSC"); sGP = sem("sGP")
    sPE = sem("sPE"); sDV = sem("sDV"); sAC = sem("sAC"); sPL = sem("sPL")
    sOD = sem("sOD")
    sBAR = sem("sBAR")

    def barrier():
        t = sBAR.inc(5)
        for e in ENGS:
            @on(e)
            def _(eng, t=t):
                eng.drain().then_inc(sBAR.h, 1)
                eng.wait_ge(sBAR.h, t)

    P = ExitStack()
    ident = sbuf(P, "ident", [128, 128], BF16)
    ident32 = sbuf(P, "ident32", [128, 128], F32)
    ones1x128_32 = sbuf(P, "ones1x128_32", [1, 128], F32)
    causadd = sbuf(P, "causadd", [128, 128], BF16)
    wineadd = sbuf(P, "wineadd", [128, 128], BF16)
    fullneg = sbuf(P, "fullneg", [128, 128], BF16)
    b32avg = sbuf(P, "b32avg", [128, 4], BF16)
    nlt16 = sbuf(P, "nlt16", [128, NB], BF16)
    z31r = sbuf(P, "z31r", [1, 128], F32)
    e65 = sbuf(P, "e65", [1, 65], BF16)
    gqv = sbuf(P, "gqv", [1, HD]); gkv = sbuf(P, "gkv", [1, HD])
    sinks = sbuf(P, "sinks", [1, 2])
    rcosf = sbuf(P, "rcosf", [128, NQT * 16]); rsinf = sbuf(P, "rsinf", [128, NQT * 16])
    rcoss = sbuf(P, "rcoss", [128, NT * 16]); rsins = sbuf(P, "rsins", [128, NT * 16])
    g2idx = sbuf(P, "g2idx", [128, 4], I32)
    wqcb = sbuf(P, "wqcb", [128, 8 * QCD], BF16)
    wqupb = sbuf(P, "wqupb", [128, 2 * 128], BF16)
    wkb = sbuf(P, "wkb", [128, 8 * HD], BF16)
    wvb = sbuf(P, "wvb", [128, 8 * HD], BF16)
    wogb = sbuf(P, "wogb", [128, 4 * 512], BF16)
    wg98 = sbuf(P, "wg98", [98, 3], BF16)
    kmaxv = sbuf(P, "kmaxv", [1, 1])
    mwg = sbuf(P, "mwg", [1, 1]); mwu = sbuf(P, "mwu", [1, 1]); mwd = sbuf(P, "mwd", [1, 1])
    kgu = sbuf(P, "kgu", [128, 1]); kdn = sbuf(P, "kdn", [128, 1])
    gq128 = sbuf(P, "gq128", [128, HD])
    gk128 = sbuf(P, "gk128", [128, HD])
    onescol32 = sbuf(P, "onescol32", [128, 1])
    m1c = sbuf(P, "m1c", [128, 1]); v1c = sbuf(P, "v1c", [128, 1])
    s1c = sbuf(P, "s1c", [128, 1]); t11 = sbuf(P, "t11", [1, 1])
    t128r = sbuf(P, "t128r", [1, 128])
    x0t = [sbuf(P, "x0t0", [128, DM]), sbuf(P, "x0t1", [128, DM])]
    xln = sbuf(P, "xln", [128, DM])
    lnsc = sbuf(P, "lnsc", [128, DM])

    _init_loads = [
        (ident[:], IDENT), (ident32[:], IDENT32), (ones1x128_32[:], ONES1x128_32C),
        (causadd[:], CAUSC), (wineadd[:], WINEC), (fullneg[:], FULLNEGC),
        (b32avg[:], B32C), (nlt16[:], NLT16C), (z31r[:], Z31RC), (e65[:], E65C),
        (gqv[:], GQN), (gkv[:], GKN), (sinks[:], MYSINK),
        (rcosf[:], RCOSF), (rsinf[:], RSINF), (rcoss[:], RCOSS), (rsins[:], RSINS),
        (g2idx[:], G2IDXC),
    ]
    import os as _os
    _init_loads = _init_loads[:int(_os.environ.get("K_INIT_N", "99"))]
    sIN.inc(16 * len(_init_loads))
    @on("sync")
    def _(s, items=_init_loads):
        for dst, src in items:
            s.dma_start(out=dst, in_=src).then_inc(sIN.h, 16)

    _gp_loads = [
        (wqcb[:].rearrange("p (a q) -> p a q", q=QCD),
         WQC.rearrange("(a p) q -> p a q", p=128)),
        (wqupb[:].rearrange("p (a q) -> p a q", q=128),
         WQUPM.rearrange("(a p) q -> p a q", p=128)),
        (wkb[:].rearrange("p (a q) -> p a q", q=HD),
         WK.rearrange("(a p) q -> p a q", p=128)),
        (wvb[:].rearrange("p (a q) -> p a q", q=HD),
         WV.rearrange("(a p) q -> p a q", p=128)),
        (wogb[:].rearrange("p (g a q) -> p g a q", g=2, q=512),
         WOG.rearrange("g (a p) q -> p g a q", p=128)),
        (wg98[0:64, :], WGATE),
        (wg98[97:98, :], BGATE),
    ]
    _gp_loads = _gp_loads[:int(_os.environ.get("K_GP_N", "99"))]
    sGP.inc(16 * len(_gp_loads))
    @on("gpsimd")
    def _(g, items=_gp_loads):
        for dst, src in items:
            g.dma_start(out=dst, in_=src).then_inc(sGP.h, 16)
    @on("vector")
    def _(v):
        v.memset(onescol32[:], 1.0).then_inc(sDV.h, 1)
    sDV.inc()
    import os as _osm
    if _osm.environ.get("K_NOMEMSET", "0") != "1":
        _w_gp0 = sGP.n
        @on("vector")
        def _(v, w=_w_gp0):
            v.wait_ge(sGP.h, w)
            v.memset(wg98[64:97, :], 0.0).then_inc(sDV.h, 1)
        sDV.inc()

    W = ExitStack()
    wqb = [sbuf(W, f"wqb{i}", [128, I4]) for i in range(2)]
    wcb = [sbuf(W, f"wcb{i}", [128, I4], BF16) for i in range(2)]
    asums = sbuf(W, "asums", [128, 32])
    asumt = sbuf(W, "asumt", [128, 1])
    wscal = sbuf(W, "wscal", [128, 1])
    wqscr = sbuf(W, "wqscr", [128, I4])
    p_t1 = psum(W, "p_t1", [1, 128])
    p_s11 = psum(W, "p_s11", [1, 1])
    p_bc = psum(W, "p_bc", [128, 1])
    p_gbq = psum(W, "p_gbq", [128, HD])
    p_gbk = psum(W, "p_gbk", [128, HD])

    import os as _osx
    if _osx.environ.get("K_NOBC", "0") != "1":
        _w_in0 = sIN.n
        _pw = sPE.inc()
        @on("tensor")
        def _(t, w=_w_in0):
            t.wait_ge(sIN.h, w)
            t.matmul(p_gbq[:], ones1x128_32[:], gqv[:], start=True, stop=True)
            t.matmul(p_gbk[:], ones1x128_32[:], gkv[:], start=True, stop=True)
            t.inc_last(sPE.h, 1)
        @on("vector")
        def _(v, w=_pw):
            v.wait_ge(sPE.h, w)
            v.tensor_copy(gq128[:], p_gbq[:])
            v.tensor_copy(gk128[:], p_gbk[:]).then_inc(sDV.h, 1)
        sDV.inc()

    def bcast_scalar(dst_col):
        w = sDV.n
        pw = sPE.inc()
        @on("tensor")
        def _(t, w=w):
            t.wait_ge(sDV.h, w)
            t.matmul(p_bc[:], ones1x128_32[:], t11[:], start=True, stop=True
                     ).then_inc(sPE.h, 1)
        @on("vector")
        def _(v, w=pw, dst=dst_col):
            v.wait_ge(sPE.h, w)
            v.tensor_copy(dst, p_bc[:]).then_inc(sDV.h, 1)
        sDV.inc()

    def quant_matrix(W_dram, C_dram, ntiles, ncols, mw_dst):
        for t in range(ntiles):
            dvw = sDV.n
            w = sIN.inc(16)
            @on("sync")
            def _(s, t=t, dvw=dvw):
                s.wait_ge(sDV.h, dvw)
                s.dma_start(out=wqb[t % 2][:, 0:ncols],
                            in_=W_dram[t * 128:(t + 1) * 128, :]).then_inc(sIN.h, 16)
            @on("vector")
            def _(v, t=t, w=w):
                v.wait_ge(sIN.h, w)
                v.reduce_sum(asums[:, t:t + 1], wqb[t % 2][:, 0:ncols], axis=AX.X,
                             apply_absolute_value=True).then_inc(sDV.h, 1)
            sDV.inc()
        @on("vector")
        def _(v, nt=ntiles):
            v.reduce_sum(asumt[:], asums[:, 0:nt], axis=AX.X).then_inc(sDV.h, 1)
        sDV.inc()
        w = sDV.n
        pw = sPE.inc()
        @on("tensor")
        def _(t, w=w):
            t.wait_ge(sDV.h, w)
            t.matmul(p_s11[:], asumt[:], onescol32[:],
                     start=True, stop=True).then_inc(sPE.h, 1)
        @on("vector")
        def _(v, w=pw, nel=float(ntiles * 128 * ncols), mw_dst=mw_dst):
            v.wait_ge(sPE.h, w)
            v.tensor_scalar(t11[:], p_s11[:], 1.0 / nel, 1e-5, ALU.mult, ALU.max)
            v.tensor_copy(mw_dst[:], t11[:])
            v.reciprocal(t11[:], t11[:]).then_inc(sDV.h, 1)
        sDV.inc()
        bcast_scalar(wscal[:])
        for t in range(ntiles):
            odw = sOD.n
            w = sIN.inc(16)
            @on("sync")
            def _(s, t=t, odw=odw):
                s.wait_ge(sOD.h, odw)
                s.dma_start(out=wqb[t % 2][:, 0:ncols],
                            in_=W_dram[t * 128:(t + 1) * 128, :]).then_inc(sIN.h, 16)
            @on("vector")
            def _(v, t=t, w=w, ncols=ncols):
                v.wait_ge(sIN.h, w)
                b = wqb[t % 2][:, 0:ncols]
                v.tensor_scalar(wqscr[:, 0:ncols], b, wscal[:, 0:1], MAGIC,
                                ALU.mult, ALU.add)
                v.tensor_scalar(wqscr[:, 0:ncols], wqscr[:, 0:ncols], MAGIC, 1.0,
                                ALU.subtract, ALU.min)
                v.tensor_scalar_max(wcb[t % 2][:, 0:ncols], wqscr[:, 0:ncols], -1.0
                                    ).then_inc(sDV.h, 1)
            sDV.inc()
            w2 = sDV.n
            sOD.inc(16)
            @on("sync")
            def _(s, t=t, w2=w2, ncols=ncols):
                s.wait_ge(sDV.h, w2)
                s.dma_start(out=C_dram[t * 128:(t + 1) * 128, :],
                            in_=wcb[t % 2][:, 0:ncols]).then_inc(sOD.h, 16)

    _scopes = []

    def _chk(tag):
        if TRUNC == tag:
            @on("sync")
            def _(se, a=sIN.n, b=sOD.n, c=sSC.n, d=sGP.n):
                se.wait_ge(sIN.h, a)
                se.wait_ge(sOD.h, b)
                se.wait_ge(sSC.h, c)
                se.wait_ge(sGP.h, d)
            for sc in reversed(_scopes):
                sc.close()
            raise _Done

    def _main():
        if TRUNC == "INIT":
            import os as _os2
            if _os2.environ.get("K_WRITEOUT", "0") == "1":
                sOD.inc(16)
                @on("sync")
                def _(s):
                    s.dma_start(out=OUT[0:128, :], in_=x0t[0][:]).then_inc(sOD.h, 16)
            W.close()
            raise _Done
        quant_matrix(WGT, WGC, 8, I4, mwg)
        quant_matrix(WUT, WUC, 8, I4, mwu)
        quant_matrix(WDT, WDC, 32, DM, mwd)
        @on("vector")
        def _(v):
            v.tensor_mul(t11[:], mwg[:], mwg[:])
            v.tensor_mul(t11[:], t11[:], mwu[:]).then_inc(sDV.h, 1)
        sDV.inc()
        bcast_scalar(kgu[:])
        @on("vector")
        def _(v):
            v.tensor_scalar_mul(t11[:], mwd[:], 0.5 / 127.0).then_inc(sDV.h, 1)
        sDV.inc()
        bcast_scalar(kdn[:])
        if "wq" in dbg:
            tw = dout("wgc", [DM, I4], BF16)
            ts_ = dout("scal", [1, 4])
            scl4 = sbuf(W, "scl4", [1, 4])
            @on("vector")
            def _(v):
                v.tensor_copy(scl4[0:1, 0:1], mwg[:])
                v.tensor_copy(scl4[0:1, 1:2], mwu[:])
                v.tensor_copy(scl4[0:1, 2:3], mwd[:]).then_inc(sDV.h, 1)
            sDV.inc()
            sOD.inc(32)
            @on("sync")
            def _(s, w=sDV.n, tw=tw, ts_=ts_):
                s.wait_ge(sDV.h, w)
                s.dma_start(out=tw, in_=WGC[:, :]).then_inc(sOD.h, 16)
                s.dma_start(out=ts_, in_=scl4[:]).then_inc(sOD.h, 16)
        barrier()
        W.close()
        _chk("W")

        def ffn_call(tag, xsrc, xdst):
            S = ExitStack()
            xq_t = [sbuf(S, f"{tag}xqt{i}", [128, DM], BF16) for i in range(BLK)]
            hbuf = sbuf(S, f"{tag}hbuf", [128, BLK * I4])
            hqt = sbuf(S, f"{tag}hqt", [128, BLK * I4], BF16)
            wgu = [sbuf(S, f"{tag}wgu{i}", [128, 2 * I4], BF16) for i in range(2)]
            wdb = [sbuf(S, f"{tag}wd{i}", [128, 512], BF16) for i in range(2)]
            alph = sbuf(S, f"{tag}alph", [128, BLK])
            beth = sbuf(S, f"{tag}beth", [128, BLK])
            hm8 = sbuf(S, f"{tag}hm8", [128, 8 * BLK])
            rsc = sbuf(S, f"{tag}rsc", [128, 512])
            r2sc = sbuf(S, f"{tag}r2sc", [128, 512])
            x1a = sbuf(S, f"{tag}x1a", [128, BLK * 512])
            hqc = sbuf(S, f"{tag}hqc", [128, I4], BF16)
            p_gu = [psum(S, f"{tag}pgu{i}", [128, 512]) for i in range(4)]
            p_tr = [psum(S, f"{tag}ptr{i}", [128, 128], BF16) for i in range(2)]

            for blk in range(4):
                tiles = [blk * BLK + i for i in range(BLK)]
                for i, ti in enumerate(tiles):
                    dvw = sDV.n
                    w = sIN.inc(16)
                    @on("sync")
                    def _(s, ti=ti, i=i, dvw=dvw):
                        s.wait_ge(sDV.h, dvw)
                        s.dma_start(out=x0t[i % 2][:],
                                    in_=xsrc[ti * 128:(ti + 1) * 128, :]).then_inc(sIN.h, 16)
                    aw = sAC.inc()
                    @on("scalar")
                    def _(a, w=w, i=i):
                        a.wait_ge(sIN.h, w)
                        a.activation(lnsc[:], x0t[i % 2][:], AF.Square, accum_out=v1c[:]
                                     ).then_inc(sAC.h, 1)
                    @on("vector")
                    def _(v, w=w, aw=aw, i=i):
                        v.wait_ge(sIN.h, w)
                        v.reduce_sum(m1c[:], x0t[i % 2][:], axis=AX.X)
                        v.tensor_scalar_mul(m1c[:], m1c[:], 1.0 / DM)
                        v.wait_ge(sAC.h, aw)
                        v.tensor_scalar_mul(v1c[:], v1c[:], 1.0 / DM)
                        v.tensor_mul(s1c[:], m1c[:], m1c[:])
                        v.tensor_sub(v1c[:], v1c[:], s1c[:])
                        v.tensor_scalar_add(v1c[:], v1c[:], 1e-5).then_inc(sDV.h, 1)
                    sDV.inc()
                    aw = sAC.inc(2)
                    @on("scalar")
                    def _(a, w=sDV.n):
                        a.wait_ge(sDV.h, w)
                        a.activation(v1c[:], v1c[:], AF.Ln)
                        a.drain()
                        a.activation(s1c[:], v1c[:], AF.Exp, scale=-0.5).then_inc(sAC.h, 2)
                    @on("vector")
                    def _(v, aw=aw, i=i):
                        v.wait_ge(sAC.h, aw)
                        v.tensor_scalar(xln[:], x0t[i % 2][:], m1c[:, 0:1], None,
                                        ALU.subtract)
                        v.tensor_scalar_mul(xln[:], xln[:], s1c[:, 0:1])
                        v.reduce_max(m1c[:], xln[:], axis=AX.X, apply_absolute_value=True)
                        v.tensor_scalar_max(m1c[:], m1c[:], 1e-5)
                        v.reciprocal(s1c[:], m1c[:])
                        v.tensor_scalar_mul(s1c[:], s1c[:], 127.0)
                        v.tensor_scalar_mul(v1c[:], m1c[:], 1.0 / 127.0)
                        v.tensor_mul(alph[:, i:i + 1], v1c[:], v1c[:])
                        v.tensor_mul(alph[:, i:i + 1], alph[:, i:i + 1], v1c[:])
                        v.tensor_mul(alph[:, i:i + 1], alph[:, i:i + 1], kgu[:, 0:1])
                        v.tensor_scalar(xln[:], xln[:], s1c[:, 0:1], MAGIC, ALU.mult, ALU.add)
                        v.tensor_scalar(xln[:], xln[:], MAGIC, 127.0, ALU.subtract, ALU.min)
                        v.tensor_scalar_max(hqc[:, 0:DM], xln[:], -128.0).then_inc(sDV.h, 1)
                    sDV.inc()
                    w2 = sDV.n
                    for k in range(8):
                        dvw = sDV.n
                        pw = sPE.inc()
                        @on("tensor")
                        def _(t, w2=w2, k=k, dvw=dvw):
                            t.wait_ge(sDV.h, max(w2, dvw - 1))
                            t.transpose(p_tr[k % 2][:], hqc[:, k * 128:(k + 1) * 128],
                                        ident[:]).then_inc(sPE.h, 1)
                        @on("vector")
                        def _(v, w=pw, i=i, k=k):
                            v.wait_ge(sPE.h, w)
                            v.tensor_copy(xq_t[i][:, k * 128:(k + 1) * 128],
                                          p_tr[k % 2][:]).then_inc(sDV.h, 1)
                        sDV.inc()
                xq_ready = sDV.n
                _hdv = []
                for ic in range(8):
                    pew = sPE.n
                    wld = sIN.inc(32)
                    @on("sync")
                    def _(s, ic=ic, pew=pew):
                        s.wait_ge(sPE.h, pew)
                        s.dma_start(
                            out=wgu[ic % 2][:, 0:I4].rearrange("p (a q) -> p a q", q=512),
                            in_=WGC.rearrange("(a p) q -> p a q", p=128)[
                                :, :, ic * 512:(ic + 1) * 512]).then_inc(sIN.h, 16)
                        s.dma_start(
                            out=wgu[ic % 2][:, I4:2 * I4].rearrange("p (a q) -> p a q", q=512),
                            in_=WUC.rearrange("(a p) q -> p a q", p=128)[
                                :, :, ic * 512:(ic + 1) * 512]).then_inc(sIN.h, 16)
                    for i in range(BLK):
                        par = (ic * BLK + i) % 2
                        pg, pu = p_gu[2 * par], p_gu[2 * par + 1]
                        gidx = ic * BLK + i
                        dvw = _hdv[gidx - 2] if gidx >= 2 else xq_ready
                        pw = sPE.inc()
                        @on("tensor")
                        def _(t, ic=ic, i=i, pg=pg, pu=pu, wld=wld, dvw=dvw,
                              xqr=xq_ready):
                            t.wait_ge(sIN.h, wld)
                            t.wait_ge(sDV.h, max(dvw, xqr))
                            for k in range(8):
                                t.matmul(pg[:], xq_t[i][:, k * 128:(k + 1) * 128],
                                         wgu[ic % 2][:, k * 512:(k + 1) * 512],
                                         start=(k == 0), stop=(k == 7))
                            for k in range(8):
                                t.matmul(pu[:], xq_t[i][:, k * 128:(k + 1) * 128],
                                         wgu[ic % 2][:, I4 + k * 512:I4 + (k + 1) * 512],
                                         start=(k == 0), stop=(k == 7))
                            t.inc_last(sPE.h, 1)
                        aw = sAC.inc()
                        @on("scalar")
                        def _(a, w=pw, pg=pg):
                            a.wait_ge(sPE.h, w)
                            a.activation(rsc[:], pg[:], AF.Relu).then_inc(sAC.h, 1)
                        @on("vector")
                        def _(v, aw=aw, i=i, ic=ic, pu=pu):
                            v.wait_ge(sAC.h, aw)
                            v.tensor_mul(r2sc[:], rsc[:], rsc[:])
                            v.scalar_tensor_tensor(
                                hbuf[:, i * I4 + ic * 512:i * I4 + (ic + 1) * 512],
                                pu[:], alph[:, i:i + 1], r2sc[:], ALU.mult, ALU.mult)
                            v.reduce_max(hm8[:, i * 8 + ic:i * 8 + ic + 1],
                                         hbuf[:, i * I4 + ic * 512:i * I4 + (ic + 1) * 512],
                                         axis=AX.X, apply_absolute_value=True
                                         ).then_inc(sDV.h, 1)
                        sDV.inc()
                        _hdv.append(sDV.n)
                for i in range(BLK):
                    @on("vector")
                    def _(v, i=i):
                        v.reduce_max(m1c[:], hm8[:, i * 8:(i + 1) * 8], axis=AX.X)
                        v.tensor_scalar_max(m1c[:], m1c[:], 1e-5)
                        v.reciprocal(s1c[:], m1c[:])
                        v.tensor_scalar_mul(s1c[:], s1c[:], 127.0)
                        v.tensor_mul(beth[:, i:i + 1], m1c[:], kdn[:, 0:1])
                        h = hbuf[:, i * I4:(i + 1) * I4]
                        v.tensor_scalar(h, h, s1c[:, 0:1], MAGIC, ALU.mult, ALU.add)
                        v.tensor_scalar(h, h, MAGIC, 127.0, ALU.subtract, ALU.min)
                        v.tensor_scalar_max(hqc[:], h, -128.0).then_inc(sDV.h, 1)
                    sDV.inc()
                    w2 = sDV.n
                    for k in range(32):
                        dvw = sDV.n
                        pw = sPE.inc()
                        @on("tensor")
                        def _(t, w2=w2, k=k, dvw=dvw):
                            t.wait_ge(sDV.h, max(w2, dvw - 1))
                            t.transpose(p_tr[k % 2][:], hqc[:, k * 128:(k + 1) * 128],
                                        ident[:]).then_inc(sPE.h, 1)
                        @on("vector")
                        def _(v, w=pw, i=i, k=k):
                            v.wait_ge(sPE.h, w)
                            v.tensor_copy(hqt[:, (i * 32 + k) * 128:(i * 32 + k + 1) * 128],
                                          p_tr[k % 2][:]).then_inc(sDV.h, 1)
                        sDV.inc()
                hq_ready = sDV.n
                for kp in range(2):
                    for it in range(32):
                        pew = sPE.n
                        wld = sIN.inc(16)
                        @on("sync")
                        def _(s, kp=kp, it=it, pew=pew):
                            s.wait_ge(sPE.h, pew)
                            s.dma_start(out=wdb[it % 2][:],
                                        in_=WDC[it * 128:(it + 1) * 128,
                                                kp * 512:(kp + 1) * 512]).then_inc(sIN.h, 16)
                        pw = sPE.inc()
                        @on("tensor")
                        def _(t, kp=kp, it=it, wld=wld, hqr=hq_ready, dvw=sDV.n):
                            t.wait_ge(sIN.h, wld)
                            if it == 0:
                                t.wait_ge(sDV.h, max(hqr, dvw))
                            for i in range(BLK):
                                t.matmul(p_gu[i][:],
                                         hqt[:, (i * 32 + it) * 128:(i * 32 + it + 1) * 128],
                                         wdb[it % 2][:], start=(it == 0), stop=(it == 31))
                            t.inc_last(sPE.h, 1)
                    pvw = sPE.n
                    odw0 = sOD.n
                    for i in range(BLK):
                        ti = tiles[i]
                        dvw = sDV.n
                        w = sIN.inc(16)
                        @on("sync")
                        def _(s, ti=ti, i=i, dvw=dvw):
                            s.wait_ge(sDV.h, dvw)
                            s.dma_start(out=x0t[i % 2][:],
                                        in_=xsrc[ti * 128:(ti + 1) * 128, :]
                                        ).then_inc(sIN.h, 16)
                        @on("vector")
                        def _(v, w=w, i=i, kp=kp, pvw=pvw, odw=odw0):
                            v.wait_ge(sIN.h, w)
                            v.wait_ge(sPE.h, pvw)
                            if kp == 1:
                                v.wait_ge(sOD.h, odw)
                            v.scalar_tensor_tensor(
                                x1a[:, i * 512:(i + 1) * 512],
                                p_gu[i][:], beth[:, i:i + 1],
                                x0t[i % 2][:, kp * 512:(kp + 1) * 512],
                                ALU.mult, ALU.add).then_inc(sDV.h, 1)
                        sDV.inc()
                    w2 = sDV.n
                    for i in range(BLK):
                        ti = tiles[i]
                        sOD.inc(16)
                        @on("sync")
                        def _(s, ti=ti, i=i, w2=w2, kp=kp):
                            s.wait_ge(sDV.h, w2)
                            s.dma_start(out=xdst[ti * 128:(ti + 1) * 128,
                                                 kp * 512:(kp + 1) * 512],
                                        in_=x1a[:, i * 512:(i + 1) * 512]
                                        ).then_inc(sOD.h, 16)
            barrier()
            S.close()

        ffn_call("f1", XIN, X1D)
        if "x1" in dbg:
            t = dout("x1", [TOK, DM])
            sOD.inc(16)
            @on("sync")
            def _(s, t=t):
                s.dma_start(out=t, in_=X1D[:, :]).then_inc(sOD.h, 16)
        _chk("F1")

        S2 = ExitStack()
        _scopes.append(S2)
        a2b = sbuf(S2, "a2b", [128, DM], BF16)
        a2t = sbuf(S2, "a2t", [128, DM], BF16)
        qlowb = sbuf(S2, "qlowb", [128, QCD], BF16)
        qlt_all = sbuf(S2, "qlt_all", [128, 2 * NT * 128], BF16)
        kt_all = sbuf(S2, "kt_all", [64, NT * 128], BF16)
        vb_all = sbuf(S2, "vb_all", [128, NT * 64], BF16)
        kct_all = sbuf(S2, "kct_all", [64, 64], BF16)
        vct_all = sbuf(S2, "vct_all", [64, 64], BF16)
        k2s = sbuf(S2, "k2s", [128, NT])
        kfb = sbuf(S2, "kfb", [128, HD])
        kscb = sbuf(S2, "kscb", [128, HD])
        knb = sbuf(S2, "knb", [128, HD], BF16)
        k2col = sbuf(S2, "k2col", [128, 1], BF16)
        rt1 = sbuf(S2, "rt1", [128, 16]); rt2 = sbuf(S2, "rt2", [128, 16])
        kcsb = sbuf(S2, "kcsb", [4, 2 * HD], BF16)
        p2_trp = [psum(S2, f"p2_trp{i}", [128, 128], BF16) for i in range(2)]
        p2_ql = psum(S2, "p2_ql", [128, QCD])
        p2_k = psum(S2, "p2_k", [128, HD])
        p2_v = psum(S2, "p2_v", [128, HD])
        p2_kc = psum(S2, "p2_kc", [4, HD])
        p2_kc4 = psum(S2, "p2_kc4", [64, 4], BF16)
        p2_t1 = psum(S2, "p2_t1", [1, 128])

        for ti in range(NT):
            dvw = sDV.n
            w = sIN.inc(16)
            @on("sync")
            def _(s, ti=ti, dvw=dvw):
                s.wait_ge(sDV.h, dvw)
                s.dma_start(out=x0t[0][:], in_=X1D[ti * 128:(ti + 1) * 128, :]
                            ).then_inc(sIN.h, 16)
            aw = sAC.inc()
            @on("scalar")
            def _(a, w=w):
                a.wait_ge(sIN.h, w)
                a.activation(lnsc[:], x0t[0][:], AF.Square, accum_out=v1c[:]
                             ).then_inc(sAC.h, 1)
            @on("vector")
            def _(v, w=w, aw=aw):
                v.wait_ge(sIN.h, w)
                v.reduce_sum(m1c[:], x0t[0][:], axis=AX.X)
                v.tensor_scalar_mul(m1c[:], m1c[:], 1.0 / DM)
                v.wait_ge(sAC.h, aw)
                v.tensor_scalar_mul(v1c[:], v1c[:], 1.0 / DM)
                v.tensor_mul(s1c[:], m1c[:], m1c[:])
                v.tensor_sub(v1c[:], v1c[:], s1c[:])
                v.tensor_scalar_add(v1c[:], v1c[:], 1e-5).then_inc(sDV.h, 1)
            sDV.inc()
            aw = sAC.inc(2)
            @on("scalar")
            def _(a, w=sDV.n):
                a.wait_ge(sDV.h, w)
                a.activation(v1c[:], v1c[:], AF.Ln)
                a.drain()
                a.activation(s1c[:], v1c[:], AF.Exp, scale=-0.5).then_inc(sAC.h, 2)
            @on("vector")
            def _(v, aw=aw):
                v.wait_ge(sAC.h, aw)
                v.tensor_scalar(xln[:], x0t[0][:], m1c[:, 0:1], None, ALU.subtract)
                v.tensor_scalar_mul(a2b[:], xln[:], s1c[:, 0:1]).then_inc(sDV.h, 1)
            sDV.inc()
            w2 = sDV.n
            for k in range(8):
                dvw = sDV.n
                pw = sPE.inc()
                @on("tensor")
                def _(t, w2=w2, k=k, dvw=dvw):
                    t.wait_ge(sDV.h, max(w2, dvw - 1))
                    t.transpose(p2_trp[k % 2][:], a2b[:, k * 128:(k + 1) * 128],
                                ident[:]).then_inc(sPE.h, 1)
                @on("vector")
                def _(v, w=pw, k=k):
                    v.wait_ge(sPE.h, w)
                    v.tensor_copy(a2t[:, k * 128:(k + 1) * 128], p2_trp[k % 2][:]
                                  ).then_inc(sDV.h, 1)
                sDV.inc()
            w2 = sDV.n
            pw = sPE.inc()
            @on("tensor")
            def _(t, w2=w2):
                t.wait_ge(sDV.h, w2)
                for k in range(8):
                    t.matmul(p2_ql[:], a2t[:, k * 128:(k + 1) * 128],
                             wqcb[:, k * QCD:(k + 1) * QCD], start=(k == 0), stop=(k == 7))
                for k in range(8):
                    t.matmul(p2_k[:], a2t[:, k * 128:(k + 1) * 128],
                             wkb[:, k * HD:(k + 1) * HD], start=(k == 0), stop=(k == 7))
                for k in range(8):
                    t.matmul(p2_v[:], a2t[:, k * 128:(k + 1) * 128],
                             wvb[:, k * HD:(k + 1) * HD], start=(k == 0), stop=(k == 7))
                t.inc_last(sPE.h, 1)
            @on("vector")
            def _(v, w=pw, ti=ti):
                v.wait_ge(sPE.h, w)
                v.tensor_copy(qlowb[:], p2_ql[:])
                v.tensor_copy(kfb[:], p2_k[:])
                v.tensor_copy(vb_all[:, ti * 64:(ti + 1) * 64], p2_v[:]
                              ).then_inc(sDV.h, 1)
            sDV.inc()
            w2 = sDV.n
            pw = sPE.inc()
            @on("tensor")
            def _(t, w2=w2):
                t.wait_ge(sDV.h, w2)
                t.transpose(p2_trp[0][:], qlowb[:, 0:128], ident[:])
                t.transpose(p2_trp[1][:], qlowb[:, 128:256], ident[:])
                t.inc_last(sPE.h, 1)
            @on("vector")
            def _(v, w=pw, ti=ti):
                v.wait_ge(sPE.h, w)
                v.tensor_copy(qlt_all[:, ti * 128:(ti + 1) * 128], p2_trp[0][:])
                v.tensor_copy(qlt_all[:, (NT + ti) * 128:(NT + ti + 1) * 128],
                              p2_trp[1][:])
                c = rcoss[:, ti * 16:(ti + 1) * 16]; sn = rsins[:, ti * 16:(ti + 1) * 16]
                v.tensor_mul(rt1[:], kfb[:, 0:16], c)
                v.tensor_mul(rt2[:], kfb[:, 16:32], sn)
                v.tensor_sub(kscb[:, 0:16], rt1[:], rt2[:])
                v.tensor_mul(rt1[:], kfb[:, 0:16], sn)
                v.tensor_mul(rt2[:], kfb[:, 16:32], c)
                v.tensor_add(kscb[:, 16:32], rt1[:], rt2[:])
                v.tensor_copy(kscb[:, 32:64], kfb[:, 32:64]).then_inc(sDV.h, 1)
            sDV.inc()
            aw = sAC.inc()
            @on("scalar")
            def _(a, w=sDV.n):
                a.wait_ge(sDV.h, w)
                a.activation(lnsc[:, 0:HD], kscb[:], AF.Square, accum_out=v1c[:]
                             ).then_inc(sAC.h, 1)
            @on("vector")
            def _(v, aw=aw):
                v.wait_ge(sAC.h, aw)
                v.tensor_scalar(v1c[:], v1c[:], 1.0 / HD, 1e-6, ALU.mult, ALU.add
                                ).then_inc(sDV.h, 1)
            sDV.inc()
            aw = sAC.inc(2)
            @on("scalar")
            def _(a, w=sDV.n):
                a.wait_ge(sDV.h, w)
                a.activation(v1c[:], v1c[:], AF.Ln)
                a.drain()
                a.activation(s1c[:], v1c[:], AF.Exp, scale=-0.5).then_inc(sAC.h, 2)
            @on("vector")
            def _(v, aw=aw, ti=ti):
                v.wait_ge(sAC.h, aw)
                v.tensor_scalar_mul(kscb[:], kscb[:], s1c[:, 0:1])
                v.tensor_mul(kscb[:], kscb[:], gk128[:])
                v.tensor_scalar_mul(knb[:], kscb[:], 0.125).then_inc(sDV.h, 1)
            sDV.inc()
            aw2 = sAC.inc()
            @on("scalar")
            def _(a, w=sDV.n, ti=ti):
                a.wait_ge(sDV.h, w)
                a.activation(lnsc[:, 0:HD], knb[:], AF.Square,
                             accum_out=k2s[:, ti:ti + 1]).then_inc(sAC.h, 1)
            w2 = sDV.n
            pw = sPE.inc()
            @on("tensor")
            def _(t, w2=w2, ti=ti):
                t.wait_ge(sDV.h, w2)
                t.transpose(p2_trp[0][0:64, :], knb[:], ident[:])
                t.matmul(p2_kc[:], b32avg[:], knb[:], start=True, stop=True)
                t.inc_last(sPE.h, 1)
            @on("vector")
            def _(v, w=pw, ti=ti):
                v.wait_ge(sPE.h, w)
                v.tensor_copy(kt_all[:, ti * 128:(ti + 1) * 128], p2_trp[0][0:64, :])
                v.tensor_copy(kcsb[:, 0:HD], p2_kc[:]).then_inc(sDV.h, 1)
            sDV.inc()
            w2 = sDV.n
            pw = sPE.inc()
            @on("tensor")
            def _(t, w2=w2, ti=ti):
                t.wait_ge(sDV.h, w2)
                t.matmul(p2_kc[:], b32avg[:], vb_all[:, ti * 64:(ti + 1) * 64],
                         start=True, stop=True).then_inc(sPE.h, 1)
            @on("vector")
            def _(v, w=pw):
                v.wait_ge(sPE.h, w)
                v.tensor_copy(kcsb[:, HD:2 * HD], p2_kc[:]).then_inc(sDV.h, 1)
            sDV.inc()
            w2 = sDV.n
            pw = sPE.inc()
            @on("tensor")
            def _(t, w2=w2):
                t.wait_ge(sDV.h, w2)
                t.transpose(p2_kc4[:], kcsb[:, 0:HD], ident[0:4, 0:4]
                            ).then_inc(sPE.h, 1)
            @on("vector")
            def _(v, w=pw, ti=ti):
                v.wait_ge(sPE.h, w)
                v.tensor_copy(kct_all[:, ti * 4:(ti + 1) * 4], p2_kc4[:]
                              ).then_inc(sDV.h, 1)
            sDV.inc()
            w2 = sDV.n
            pw = sPE.inc()
            @on("tensor")
            def _(t, w2=w2):
                t.wait_ge(sDV.h, w2)
                t.transpose(p2_kc4[:], kcsb[:, HD:2 * HD], ident[0:4, 0:4]
                            ).then_inc(sPE.h, 1)
            @on("vector")
            def _(v, w=pw, ti=ti):
                v.wait_ge(sPE.h, w)
                v.tensor_copy(vct_all[:, ti * 4:(ti + 1) * 4], p2_kc4[:]
                              ).then_inc(sDV.h, 1)
            sDV.inc()
        @on("vector")
        def _(v, w=sAC.n):
            v.wait_ge(sAC.h, w)
            v.reduce_max(m1c[:], k2s[:], axis=AX.X)
            v.tensor_copy(k2col[:], m1c[:]).then_inc(sDV.h, 1)
        sDV.inc()
        w2 = sDV.n
        sOD.inc(7 * 16)
        @on("sync")
        def _(s, w2=w2):
            s.wait_ge(sDV.h, w2)
            for hh in range(2):
                s.dma_start(out=CONTRIB[SEC_QL + hh * 2048:SEC_QL + (hh + 1) * 2048,
                                        :].rearrange("(q t) c -> q t c", t=NT),
                            in_=qlt_all[:, hh * 2048:(hh + 1) * 2048].rearrange(
                                "p (t c) -> p t c", c=128)).then_inc(sOD.h, 16)
            s.dma_start(out=CONTRIB[SEC_K:SEC_K + 1024, :].rearrange(
                            "(d t) c -> d t c", t=NT),
                        in_=kt_all[:].rearrange("p (t c) -> p t c", c=128)
                        ).then_inc(sOD.h, 16)
            s.dma_start(out=CONTRIB[SEC_V:SEC_V + 1024, :].rearrange(
                            "(p tt) c -> p tt c", tt=8),
                        in_=vb_all[:].rearrange("p (tt c) -> p tt c", c=128)
                        ).then_inc(sOD.h, 16)
            s.dma_start(out=CONTRIB[SEC_KC:SEC_KC + 64, 0:64], in_=kct_all[:]
                        ).then_inc(sOD.h, 16)
            s.dma_start(out=CONTRIB[SEC_VC:SEC_VC + 64, 0:64], in_=vct_all[:]
                        ).then_inc(sOD.h, 16)
            s.dma_start(out=CONTRIB[SEC_K2:SEC_K2 + 1, :].rearrange("o c -> c o"),
                        in_=k2col[:]).then_inc(sOD.h, 16)
        if "k2s" in dbg:
            t = dout("k2s", [128, NT])
            sOD.inc(16)
            @on("sync")
            def _(s, t=t, w=sDV.n):
                s.wait_ge(sDV.h, w)
                s.dma_start(out=t, in_=k2s[:]).then_inc(sOD.h, 16)
        barrier()
        S2.close()
        _chk("P2")
        gw = sGP.inc()
        @on("gpsimd")
        def _(g, w=sOD.n):
            g.wait_ge(sOD.h, w)
            g.collective_compute("AllGather", ALU.bypass, replica_groups=RG,
                                 ins=[CONTRIB.opt()], outs=[GOUT.opt()]
                                 ).then_inc(sGP.h, 1)

        A = ExitStack()
        _scopes.append(A)
        QN = ExitStack()
        _scopes.append(QN)
        kstack = sbuf(A, "kstack", [98, SEQ], BF16)
        ksw = sbuf(A, "ksw", [98, SEQ], BF16)
        vones = sbuf(A, "vones", [128, NQT * 65], BF16)
        kc98 = sbuf(A, "kc98", [98, 512], BF16)
        vc1 = [sbuf(A, f"vc1_{i}", [128, 65], BF16) for i in range(2)]
        vctg = sbuf(A, "vctg", [64, 256], BF16)
        k2g = sbuf(A, "k2g", [1, 512], BF16)
        qa_h = [sbuf(A, f"qa{h}", [98, SEQ], BF16) for h in range(2)]
        qb_sh = sbuf(A, "qb_sh", [98, SEQ], BF16)
        qab = [[qa_h[0], qb_sh], [qa_h[1], qb_sh]]
        kmb = sbuf(A, "kmb", [128, 1])
        mxts = sbuf(A, "mxts", [128, SEQ], BF16)
        qlt0 = sbuf(QN, "qlt0", [128, SEQ], BF16)
        qlt1 = sbuf(QN, "qlt1", [128, SEQ], BF16)
        mrowb = sbuf(QN, "mrowb", [2, SEQ], BF16)

        G = GOUT.rearrange("(r n) c -> r n c", n=NCB)
        ldw0 = sIN.n
        sIN.inc(23 * 16)
        @on("sync")
        def _(s, w=sGP.n):
            s.wait_ge(sGP.h, w)
            s.dma_start(out=qlt0[:].rearrange("q (r t c) -> q r t c", r=4, c=128),
                        in_=G[:, SEC_QL:SEC_QL + 2048, :].rearrange(
                            "r (q t) c -> q r t c", t=NT)).then_inc(sIN.h, 16)
            s.dma_start(out=qlt1[:].rearrange("q (r t c) -> q r t c", r=4, c=128),
                        in_=G[:, SEC_QL + 2048:SEC_QL + 4096, :].rearrange(
                            "r (q t) c -> q r t c", t=NT)).then_inc(sIN.h, 16)
            kin = G[:, SEC_K:SEC_K + 1024, :].rearrange("r (d t) c -> d r t c", t=NT)
            s.dma_start(out=kstack[0:64, :].rearrange("d (r t c) -> d r t c", r=4, c=128),
                        in_=kin).then_inc(sIN.h, 16)
            s.dma_start(out=ksw[0:64, :].rearrange("d (r t c) -> d r t c", r=4, c=128),
                        in_=kin).then_inc(sIN.h, 16)
            s.dma_start(out=kstack[64:96, :], in_=R32BIGC).then_inc(sIN.h, 16)
            s.dma_start(out=kstack[96:97, :], in_=ONESROWC).then_inc(sIN.h, 16)
            s.dma_start(out=kstack[97:98, :], in_=ZROWC).then_inc(sIN.h, 16)
            s.dma_start(out=ksw[64:96, :], in_=Z32RC).then_inc(sIN.h, 16)
            s.dma_start(out=ksw[96:97, :], in_=ONESROWC).then_inc(sIN.h, 16)
            s.dma_start(out=ksw[97:98, :], in_=ZROWC).then_inc(sIN.h, 16)
            for rr in range(4):
                for t2 in range(2):
                    s.dma_start(
                        out=vones[:].rearrange("p (r tt t2 x) -> p r tt t2 x",
                                               r=4, tt=8, x=65)[:, rr, :, t2, 0:64],
                        in_=G[rr, SEC_V:SEC_V + 1024, :].rearrange(
                            "(p tt) (t2 d) -> p tt t2 d", tt=8, t2=2)[:, :, t2, :]
                        ).then_inc(sIN.h, 16)
            s.dma_start(out=kc98[0:64, 0:NB].rearrange("d (r c) -> d r c", r=4),
                        in_=G[:, SEC_KC:SEC_KC + 64, 0:64].rearrange(
                            "r d c -> d r c")).then_inc(sIN.h, 16)
            s.dma_start(out=kc98[96:97, 0:NB], in_=ONESROWC[:, 0:NB]).then_inc(sIN.h, 16)
            s.dma_start(out=kc98[97:98, 0:NB], in_=ZROWC[:, 0:NB]).then_inc(sIN.h, 16)
            s.dma_start(out=vctg[:].rearrange("d (r c) -> d r c", r=4),
                        in_=G[:, SEC_VC:SEC_VC + 64, 0:64].rearrange(
                            "r d c -> d r c")).then_inc(sIN.h, 16)
            s.dma_start(out=k2g[:].rearrange("o (r c) -> o r c", r=4),
                        in_=G[:, SEC_K2:SEC_K2 + 1, :].rearrange(
                            "r o c -> o r c")).then_inc(sIN.h, 16)
        _pw = sPE.inc(2)
        _dv2 = sDV.inc(2)
        @on("tensor")
        def _(t, w=sIN.n, pw=_pw, dv0=_dv2 - 1):
            t.wait_ge(sIN.h, w)
            t.transpose(pq_vt[:], vctg[:, 0:128], ident[0:64, 0:64]).then_inc(sPE.h, 1)
            t.wait_ge(sDV.h, dv0)
            t.transpose(pq_vt[:], vctg[:, 128:256], ident[0:64, 0:64]).then_inc(sPE.h, 1)
        @on("vector")
        def _(v, pw=_pw):
            v.wait_ge(sPE.h, pw - 1)
            v.tensor_copy(vc1[0][:, 0:64], pq_vt[:])
            v.memset(vc1[0][:, 64:65], 1.0).then_inc(sDV.h, 1)
            v.wait_ge(sPE.h, pw)
            v.tensor_copy(vc1[1][:, 0:64], pq_vt[:])
            v.memset(vc1[1][:, 64:65], 1.0).then_inc(sDV.h, 1)
        @on("vector")
        def _(v, w=sIN.n):
            v.wait_ge(sIN.h, w)
            v.reduce_max(t11[:], k2g[:], axis=AX.X).then_inc(sDV.h, 1)
        sDV.inc()
        aw = sAC.inc(2)
        @on("scalar")
        def _(a, w=sDV.n):
            a.wait_ge(sDV.h, w)
            a.activation(t11[:], t11[:], AF.Ln)
            a.drain()
            a.activation(t11[:], t11[:], AF.Exp, scale=0.5).then_inc(sAC.h, 2)
        @on("vector")
        def _(v, aw=aw):
            v.wait_ge(sAC.h, aw)
            v.tensor_scalar_mul(kmaxv[:], t11[:], 1.02)
            v.memset(vones[:].rearrange("p (t x) -> p t x", x=65)[:, :, 64:65], 1.0)
            v.memset(kc98[0:64, NB:512], 0.0)
            v.memset(kc98[64:96, :], 0.0).then_inc(sDV.h, 1)
        sDV.inc()
        kmax_ready = sDV.n

        qf = sbuf(QN, "qf", [128, 512])
        qsq = sbuf(QN, "qsq", [128, 512])
        ss2 = sbuf(QN, "ss2", [128, 8])
        qnb = sbuf(QN, "qnb", [128, 512], BF16)
        rr1 = sbuf(QN, "rr1", [128, 128]); rr2 = sbuf(QN, "rr2", [128, 128])
        pq_qa = psum(QN, "pq_qa", [128, 128])
        pq_qb = psum(QN, "pq_qb", [128, 128])
        pq_ta = psum(QN, "pq_ta", [128, 128], BF16)
        pq_tb = psum(QN, "pq_tb", [128, 128], BF16)
        pq_vt = psum(QN, "pq_vt", [128, 64], BF16)
        pq_k1 = psum(QN, "pq_k1", [128, 1])
        pq_m1 = psum(QN, "pq_m1", [2, 128])

        _pw = sPE.inc()
        @on("tensor")
        def _(t, w=sDV.n):
            t.wait_ge(sDV.h, w)
            t.matmul(pq_k1[:], ones1x128_32[:], kmaxv[:], start=True, stop=True
                     ).then_inc(sPE.h, 1)
        @on("vector")
        def _(v, w=_pw):
            v.wait_ge(sPE.h, w)
            v.tensor_copy(kmb[:], pq_k1[:]).then_inc(sDV.h, 1)
        sDV.inc()
        kmb_ready = sDV.n

        if "k2g" in dbg:
            t = dout("k2g", [1, 512], BF16)
            t2 = dout("kmx", [1, 1])
            sOD.inc(32)
            @on("sync")
            def _(s, t=t, t2=t2, w=kmb_ready):
                s.wait_ge(sDV.h, w)
                s.dma_start(out=t, in_=k2g[:]).then_inc(sOD.h, 16)
                s.dma_start(out=t2, in_=kmaxv[:]).then_inc(sOD.h, 16)
        _chk("PG")
        import os as _osq
        _qn_n = int(_osq.environ.get("K_QN_N", str(NQT // 4)))
        _qn_skip = set(_osq.environ.get("K_QN_SKIP", "").split(","))
        for q4 in range(_qn_n):
            qts = [q4 * 4 + i for i in range(4)]
            dvw = sDV.n
            ev = sDV.inc(4)
            pw = sPE.inc(4)
            @on("tensor")
            def _(t, qts=qts, w=ldw0 + 2 * 16, dvw=dvw, ev=ev):
                t.wait_ge(sIN.h, w)
                if qts[0] > 0:
                    t.wait_ge(sDV.h, dvw)
                for i, qt in enumerate(qts):
                    dst = pq_qa if i % 2 == 0 else pq_qb
                    if i >= 2:
                        t.wait_ge(sDV.h, ev - 3 + (i - 2))
                    t.matmul(dst[:],
                             qlt0[:, qt * 128:(qt + 1) * 128], wqupb[:, 0:128],
                             start=True, stop=False)
                    t.matmul(dst[:],
                             qlt1[:, qt * 128:(qt + 1) * 128], wqupb[:, 128:256],
                             start=False, stop=True)
                    t.inc_last(sPE.h, 1)
            @on("vector")
            def _(v, w=pw, qts=qts):
                for i in range(4):
                    v.wait_ge(sPE.h, w - 3 + i)
                    v.tensor_copy(qf[:, i * 128:(i + 1) * 128],
                                  pq_qa[:] if i % 2 == 0 else pq_qb[:])
                    v.inc_last(sDV.h, 1)
                cv = rcosf[:, qts[0] * 16:(qts[0] + 4) * 16].rearrange(
                    "p (t o x) -> p t o x", o=1, x=16).to_broadcast([128, 4, 2, 16])
                sv = rsinf[:, qts[0] * 16:(qts[0] + 4) * 16].rearrange(
                    "p (t o x) -> p t o x", o=1, x=16).to_broadcast([128, 4, 2, 16])
                q1 = qf[:].rearrange("p (t h x) -> p t h x", h=2, x=64)[:, :, :, 0:16]
                q2 = qf[:].rearrange("p (t h x) -> p t h x", h=2, x=64)[:, :, :, 16:32]
                r1v = rr1[:].rearrange("p (t h x) -> p t h x", h=2, x=16)
                r2v = rr2[:].rearrange("p (t h x) -> p t h x", h=2, x=16)
                v.tensor_mul(r1v, q1, cv)
                v.tensor_mul(r2v, q2, sv)
                v.tensor_sub(r1v, r1v, r2v)
                v.tensor_mul(r2v, q2, cv)
                v.tensor_mul(q2, q1, sv)
                v.tensor_add(q2, q2, r2v)
                v.tensor_copy(q1, r1v).then_inc(sDV.h, 1)
            sDV.inc()
            aw = sAC.inc()
            @on("scalar")
            def _(a, w=sDV.n):
                a.wait_ge(sDV.h, w)
                a.activation(qsq[:], qf[:], AF.Square).then_inc(sAC.h, 1)
            @on("vector")
            def _(v, aw=aw):
                v.wait_ge(sAC.h, aw)
                v.reduce_sum(ss2[:], qsq[:].rearrange("p (g x) -> p g x", x=64), axis=AX.X)
                v.tensor_scalar(ss2[:], ss2[:], 1.0 / HD, 1e-6, ALU.mult, ALU.add
                                ).then_inc(sDV.h, 1)
            sDV.inc()
            aw = sAC.inc(2)
            @on("scalar")
            def _(a, w=sDV.n):
                a.wait_ge(sDV.h, w)
                a.activation(ss2[:], ss2[:], AF.Ln)
                a.drain()
                a.activation(ss2[:], ss2[:], AF.Exp, scale=-0.5).then_inc(sAC.h, 2)
            @on("vector")
            def _(v, aw=aw):
                v.wait_ge(sAC.h, aw)
                v.tensor_mul(qf[:], qf[:],
                             ss2[:].rearrange("p (g o) -> p g o", o=1).to_broadcast([128, 8, 64]))
                v.tensor_mul(qnb[:].rearrange("p (g x) -> p g x", x=64),
                             qf[:].rearrange("p (g x) -> p g x", x=64),
                             gq128[:].rearrange("p (o x) -> p o x", o=1).to_broadcast([128, 8, 64])
                             ).then_inc(sDV.h, 1)
            sDV.inc()
            aw = sAC.inc()
            @on("scalar")
            def _(a, w=sDV.n):
                a.wait_ge(sDV.h, w)
                a.activation(qsq[:], qnb[:], AF.Square).then_inc(sAC.h, 1)
            @on("vector")
            def _(v, aw=aw):
                v.wait_ge(sAC.h, aw)
                v.reduce_sum(ss2[:], qsq[:].rearrange("p (g x) -> p g x", x=64), axis=AX.X
                             ).then_inc(sDV.h, 1)
            sDV.inc()
            aw = sAC.inc(2)
            @on("scalar")
            def _(a, w=sDV.n):
                a.wait_ge(sDV.h, w)
                a.activation(ss2[:], ss2[:], AF.Ln)
                a.drain()
                a.activation(ss2[:], ss2[:], AF.Exp, scale=0.5).then_inc(sAC.h, 2)
            @on("vector")
            def _(v, aw=aw):
                v.wait_ge(sAC.h, aw)
                v.scalar_tensor_tensor(ss2[:], ss2[:], -1.0,
                                       kmb[:, 0:1].to_broadcast([128, 8]),
                                       ALU.mult, ALU.mult).then_inc(sDV.h, 1)
            sDV.inc()
            w2 = sDV.n
            ev = sDV.inc(4)
            pw = sPE.inc(4)
            @on("tensor")
            def _(t, qts=qts, w2=w2, ev=ev):
                t.wait_ge(sDV.h, w2)
                for i in range(4):
                    dst = pq_ta if i % 2 == 0 else pq_tb
                    if i >= 2:
                        t.wait_ge(sDV.h, ev - 3 + (i - 2))
                    t.transpose(dst[:], qnb[:, i * 128:(i + 1) * 128], ident[:])
                    t.inc_last(sPE.h, 1)
            @on("vector")
            def _(v, qts=qts, w=pw):
                for i in range(4):
                    v.wait_ge(sPE.h, w - 3 + i)
                    src = pq_ta if i % 2 == 0 else pq_tb
                    for hl in range(2):
                        v.tensor_copy(
                            qab[hl][0][0:64, (qts[0] + i) * 128:(qts[0] + i + 1) * 128],
                            src[hl * 64:(hl + 1) * 64, :])
                    v.inc_last(sDV.h, 1)
            if "mt" not in _qn_skip:
                for i, qt in enumerate(qts):
                    w2 = sDV.n
                    pw = sPE.inc()
                    @on("tensor")
                    def _(t, i=i, w2=w2):
                        t.wait_ge(sDV.h, w2)
                        t.transpose(pq_m1[:], ss2[:, i * 2:(i + 1) * 2], ident32[:]
                                    ).then_inc(sPE.h, 1)
                    @on("vector")
                    def _(v, qt=qt, w=pw):
                        v.wait_ge(sPE.h, w)
                        v.tensor_copy(mrowb[0:2, qt * 128:(qt + 1) * 128], pq_m1[:]
                                      ).then_inc(sDV.h, 1)
                    sDV.inc()
        w2 = sDV.n
        sOD.inc(16)
        @on("sync")
        def _(s, w2=w2):
            s.wait_ge(sDV.h, w2)
            s.dma_start(out=MROWD[:, :], in_=mrowb[:]).then_inc(sOD.h, 16)
        odw = sOD.n
        sIN.inc(6 * 16)
        @on("sync")
        def _(s, w=odw):
            s.wait_ge(sOD.h, w)
            for hl in range(2):
                s.dma_start(out=qa_h[hl][96:97, :], in_=MROWD[hl:hl + 1, :]
                            ).then_inc(sIN.h, 16)
                s.dma_start(out=qa_h[hl][97:98, :], in_=ONESROWC).then_inc(sIN.h, 16)
            s.dma_start(out=qb_sh[96:97, :], in_=MROWD[0:1, :]).then_inc(sIN.h, 16)
            s.dma_start(out=qb_sh[97:98, :], in_=ONESROWC).then_inc(sIN.h, 16)
        sIN.inc(16)
        @on("sync")
        def _(s, w=sDV.n):
            s.wait_ge(sDV.h, w)
            s.dma_start(out=qb_sh[0:64, :], in_=qa_h[0][0:64, :]).then_inc(sIN.h, 16)
        q_ready = sIN.n
        barrier()
        if "qq0" in dbg:
            t = dout("qq0", [98, SEQ], BF16)
            sOD.inc(16)
            @on("sync")
            def _(s, t=t):
                s.dma_start(out=t, in_=qab[0][0][:]).then_inc(sOD.h, 16)
        if "qq1" in dbg:
            t = dout("qq1", [98, SEQ], BF16)
            sOD.inc(16)
            @on("sync")
            def _(s, t=t):
                s.dma_start(out=t, in_=qab[1][0][:]).then_inc(sOD.h, 16)
        if "kst" in dbg:
            t = dout("kst", [98, SEQ], BF16)
            sOD.inc(16)
            @on("sync")
            def _(s, t=t):
                s.dma_start(out=t, in_=kstack[:]).then_inc(sOD.h, 16)
        QN.close()
        _chk("QN")

        AT = ExitStack()
        _scopes.append(AT)
        selt = [sbuf(AT, f"selt{i}", [64, 512], BF16) for i in range(4)]
        vaddq4 = sbuf(AT, "vaddq4", [128, 4 * NB], BF16)
        vaddt2 = [sbuf(AT, f"vaddt2_{i}", [128, 512], BF16) for i in range(2)]
        scqm = sbuf(AT, "scqm", [128, NB])
        knk = sbuf(AT, "knk", [128, NB])
        top8 = sbuf(AT, "top8", [128, 16])
        selm = sbuf(AT, "selm", [128, 4 * NB])
        csc = [sbuf(AT, f"csc{i}", [128, 512]) for i in range(2)]
        cexp = [sbuf(AT, f"cexp{i}", [128, 512], BF16) for i in range(2)]
        expb = [sbuf(AT, f"expb{i}", [128, 1024], BF16) for i in range(2)]
        cswsb = sbuf(AT, "cswsb", [128, 256])
        exsw = sbuf(AT, "exsw", [128, 640], BF16)
        esink = sbuf(AT, "esink", [1, 512], BF16)
        compsb = sbuf(AT, "compsb", [65, 512])
        selsb = sbuf(AT, "selsb", [65, 512])
        swsb = sbuf(AT, "swsb", [65, 512])
        g3r = [sbuf(AT, f"g3r{i}", [1, 512]) for i in range(3)]
        g3rb = [sbuf(AT, f"g3rb{i}", [1, 512], BF16) for i in range(3)]
        gnr = sbuf(AT, "gnr", [1, 512])
        t1x = sbuf(AT, "t1x", [1, 512])
        onesr64 = sbuf(AT, "onesr64", [1, 64], BF16)
        pS = [psum(AT, f"pS{i}", [128, 512]) for i in range(4)]
        pP = psum(AT, "pP", [128, 512])
        pT = [psum(AT, f"pT{i}", [128, 128]) for i in range(3)]

        sQA = sem("sQA"); sQE = sem("sQE"); sQD = sem("sQD"); sQP = sem("sQP")

        @on("vector")
        def _(v):
            v.memset(onesr64[:], 1.0).then_inc(sQD.h, 1)
        sQD.inc()

        swT_free = 0
        for hl in range(2):
            if hl == 1:
                sIN.inc(32)
                @on("sync")
                def _(s, aw=sQA.n, plw=sPL.n, pvw=sQP.n):
                    s.wait_ge(sQA.h, aw)
                    s.wait_ge(sQP.h, pvw)
                    s.wait_ge(sPL.h, plw)
                    s.dma_start(out=qb_sh[0:64, :], in_=qa_h[1][0:64, :]
                                ).then_inc(sIN.h, 16)
                    s.dma_start(out=qb_sh[96:97, :], in_=MROWD[1:2, :]).then_inc(sIN.h, 16)
                q_ready = sIN.n
            for qc in range(16):
                qts = [qc * 4 + i for i in range(4)]
                ktmax = 4 * qc + 4
                QA, QB = qab[hl][0], qab[hl][1]
                qsl = slice(qc * 512, (qc + 1) * 512)
                rdv = sQD.n; rqe = sQE.n; rqa = sQA.n; rqp = sQP.n; rpl = sPL.n
                w0 = sIN.inc(48)
                @on("sync")
                def _(s, qts=qts, dvw=rdv):
                    s.wait_ge(sQD.h, dvw)
                    s.dma_start(out=vaddq4[:],
                                in_=VADDQC[:, qts[0] * NB:(qts[0] + 4) * NB]
                                ).then_inc(sIN.h, 16)
                    s.dma_start(out=vaddt2[0][:],
                                in_=VADDTC[0:128, qts[0] * 128:(qts[0] + 4) * 128]
                                ).then_inc(sIN.h, 16)
                    s.dma_start(out=vaddt2[1][:],
                                in_=VADDTC[128:256, qts[0] * 128:(qts[0] + 4) * 128]
                                ).then_inc(sIN.h, 16)
                if qc == 0:
                    @on("vector")
                    def _(v, w=w0):
                        v.wait_ge(sIN.h, w)
                        v.tensor_copy(selm[:], nlt16[:].rearrange("p (o c) -> p o c", o=1
                                      ).to_broadcast([128, 4, NB])).then_inc(sQD.h, 1)
                    sQD.inc()
                else:
                    tk0 = sQD.n
                    for i in range(4):
                        pw = sQA.inc()
                        @on("tensor")
                        def _(t, i=i, qts=qts, w=w0, qr=q_ready, rdv=rdv, rqe=rqe,
                              tk0=tk0, QA=QA):
                            t.wait_ge(sIN.h, max(w, qr))
                            if i < 2:
                                t.wait_ge(sQD.h, rdv)
                                t.wait_ge(sQE.h, rqe)
                            else:
                                t.wait_ge(sQD.h, tk0 + i - 1)
                            t.matmul(pS[2 + i % 2][:],
                                     QA[0:64, qts[i] * 128:(qts[i] + 1) * 128],
                                     kc98[0:64, 0:512], start=True, stop=True
                                     ).then_inc(sQA.h, 1)
                        @on("vector")
                        def _(v, i=i, w=pw):
                            v.wait_ge(sQA.h, w)
                            v.tensor_add(scqm[:], pS[2 + i % 2][:, 0:NB],
                                         vaddq4[:, i * NB:(i + 1) * NB])
                            v.drain()
                            v.max(out=top8[:, 0:8], in_=scqm[:])
                            v.tensor_scalar(knk[:], scqm[:],
                                            top8[:, 7:8], -65536.0, ALU.is_ge, ALU.mult)
                            v.tensor_add(knk[:], knk[:], scqm[:])
                            v.drain()
                            v.max(out=top8[:, 8:16], in_=knk[:])
                            v.tensor_scalar(selm[:, i * NB:(i + 1) * NB],
                                            scqm[:], top8[:, 15:16], 1.0,
                                            ALU.is_ge, ALU.subtract)
                            v.inc_last(sQD.h, 1)
                        sQD.inc()
                w2 = sQD.n
                tb0 = sQD.n
                pw8 = sQA.inc(8)
                @on("tensor")
                def _(t, w2=w2, rqe=rqe, tb0=tb0):
                    t.wait_ge(sQD.h, w2)
                    t.wait_ge(sQE.h, rqe)
                    for j in range(8):
                        if j >= 2:
                            t.wait_ge(sQD.h, tb0 + j - 1)
                        t.transpose(pT[j % 2][:], selm[:, j * 128:(j + 1) * 128],
                                    ident32[:])
                        t.inc_last(sQA.h, 1)
                @on("vector")
                def _(v, pw8=pw8, rpl=rpl):
                    v.wait_ge(sPL.h, rpl)
                    for j in range(8):
                        v.wait_ge(sQA.h, pw8 - 7 + j)
                        i, h = j // 2, j % 2
                        src = pT[j % 2]
                        v.tensor_copy(selt[2 * h][:, i * 128:(i + 1) * 128], src[0:64, :])
                        v.tensor_copy(selt[2 * h + 1][:, i * 128:(i + 1) * 128],
                                      src[64:128, :])
                        v.inc_last(sQD.h, 1)
                sQD.inc(8)
                selt_ready = sQD.n
                bank_free = [("D", tb0 + 7), ("D", tb0 + 8)]
                pwc = sQA.inc()
                @on("tensor")
                def _(t, w=w0, qr=q_ready, rdv=rdv, rqe=rqe, qsl=qsl, QA=QA):
                    t.wait_ge(sIN.h, max(w, qr))
                    t.wait_ge(sQD.h, rdv)
                    t.wait_ge(sQE.h, rqe)
                    t.matmul(pS[0][:], kc98[:, 0:128], QA[:, qsl], start=True, stop=True)
                    t.matmul(pS[1][:], kc98[:, 128:256], QA[:, qsl], start=True, stop=True)
                    t.inc_last(sQA.h, 1)
                aw = sAC.inc()
                @on("scalar")
                def _(a, hl=hl, qsl=qsl, qr=q_ready, rqp=rqp, rqa=rqa, QA=QA):
                    a.wait_ge(sIN.h, qr)
                    a.wait_ge(sQP.h, rqp)
                    a.wait_ge(sQA.h, rqa)
                    a.activation(esink[:], QA[96:97, qsl], AF.Exp,
                                 bias=sinks[0:1, hl:hl + 1]).then_inc(sAC.h, 1)
                esink_ready = sAC.n
                @on("vector")
                def _(v, w=pwc):
                    v.wait_ge(sQA.h, w)
                    v.tensor_add(csc[0][:], pS[0][:], vaddt2[0][:])
                    v.tensor_add(csc[1][:], pS[1][:], vaddt2[1][:])
                    v.inc_last(sQD.h, 1)
                sQD.inc()
                csc_done = sQD.n
                ce = sQE.inc()
                @on("scalar")
                def _(a, w=csc_done, rqp=rqp):
                    a.wait_ge(sQD.h, w)
                    a.wait_ge(sQP.h, rqp)
                    a.activation(cexp[0][:], csc[0][:], AF.Exp)
                    a.activation(cexp[1][:], csc[1][:], AF.Exp)
                    a.inc_last(sQE.h, 1)
                cexp_ready = sQE.n
                pvc = sQP.inc()
                @on("tensor")
                def _(t, w=cexp_ready, ew=esink_ready, rdv=rdv):
                    t.wait_ge(sQE.h, w)
                    t.wait_ge(sAC.h, ew)
                    t.wait_ge(sQD.h, rdv)
                    t.matmul(pP[0:65, :], vc1[0][:], cexp[0][:], start=True, stop=False)
                    t.matmul(pP[0:65, :], vc1[1][:], cexp[1][:], start=False, stop=False)
                    t.matmul(pP[0:65, :], e65[:], esink[:], start=False, stop=True)
                    t.inc_last(sQP.h, 1)
                @on("vector")
                def _(v, w=pvc):
                    v.wait_ge(sQP.h, w)
                    v.tensor_copy(compsb[:], pP[0:65, :]).then_inc(sQD.h, 1)
                sQD.inc()
                comp_evac = sQD.n
                npairs = ktmax // 2
                exp_inc = {}
                pv_inc = {}
                for ch in range((ktmax + 7) // 8):
                    @on("gpsimd")
                    def _(g, ch=ch, sr=selt_ready, aw=sQA.n, QA=QA, QB=QB, qsl=qsl):
                        g.wait_ge(sQD.h, sr)
                        g.wait_ge(sQA.h, aw)
                        dst = (QA if ch % 2 == 0 else QB)
                        g.tensor_copy(dst[64:96, qsl],
                                      selt[ch // 2][(ch % 2) * 32:(ch % 2) * 32 + 32, :]
                                      ).then_inc(sPL.h, 1)
                    sPL.inc()
                    pl_ch = sPL.n
                    for g_ in range(ch * 4, min(npairs, ch * 4 + 4)):
                        kts = [2 * g_, 2 * g_ + 1]
                        par = g_ % 2
                        pw = sQA.inc()
                        @on("tensor")
                        def _(t, g_=g_, kts=kts, par=par, pl=pl_ch, ch=ch,
                              QA=QA, QB=QB, qsl=qsl, cd=csc_done, rqe=rqe,
                              ei=exp_inc.get(g_ - 2, 0)):
                            t.wait_ge(sPL.h, pl)
                            if g_ < 2:
                                t.wait_ge(sQD.h, cd)
                                t.wait_ge(sQE.h, rqe)
                            else:
                                t.wait_ge(sQE.h, ei)
                            src = (QA if ch % 2 == 0 else QB)
                            for j, kt in enumerate(kts):
                                t.matmul(pS[2 * par + j][:],
                                         kstack[:, kt * 128:(kt + 1) * 128],
                                         src[:, qsl], start=True, stop=True)
                            t.inc_last(sQA.h, 1)
                        if qc == 0:
                            @on("vector")
                            def _(v, g_=g_, kts=kts, par=par, w=pw, ce=cexp_ready,
                                  pe=exp_inc.get(g_ - 1, 0)):
                                v.wait_ge(sQA.h, w)
                                v.wait_ge(sQE.h, max(ce, pe))
                                for j, kt in enumerate(kts):
                                    for i in range(4):
                                        dst = csc[j][:, i * 128:(i + 1) * 128]
                                        srcp = pS[2 * par + j][:, i * 128:(i + 1) * 128]
                                        if kt == i:
                                            v.tensor_add(dst, srcp, causadd[:])
                                        elif kt > i:
                                            v.tensor_add(dst, srcp, fullneg[:])
                                        else:
                                            v.tensor_copy(dst, srcp)
                                v.inc_last(sQD.h, 1)
                            sQD.inc()
                            mw = sQD.n
                            sQE.inc()
                            @on("scalar")
                            def _(a, par=par, w=mw):
                                a.wait_ge(sQD.h, w)
                                a.activation(expb[par][:, 0:512], csc[0][:], AF.Exp)
                                a.activation(expb[par][:, 512:1024], csc[1][:], AF.Exp)
                                a.inc_last(sQE.h, 1)
                        else:
                            sQE.inc()
                            @on("scalar")
                            def _(a, par=par, w=pw, pvw=pv_inc.get(g_ - 2, 0)):
                                a.wait_ge(sQA.h, w)
                                if pvw:
                                    a.wait_ge(sQP.h, pvw)
                                a.activation(expb[par][:, 0:512], pS[2 * par][:], AF.Exp)
                                a.activation(expb[par][:, 512:1024], pS[2 * par + 1][:],
                                             AF.Exp)
                                a.inc_last(sQE.h, 1)
                        exp_inc[g_] = sQE.n
                        pv = sQP.inc()
                        @on("tensor")
                        def _(t, g_=g_, kts=kts, par=par, w=exp_inc[g_],
                              ce=comp_evac, npairs=npairs):
                            t.wait_ge(sQE.h, w)
                            if g_ == 0:
                                t.wait_ge(sQD.h, ce)
                            for j, kt in enumerate(kts):
                                t.matmul(pP[0:65, :], vones[:, kt * 65:(kt + 1) * 65],
                                         expb[par][:, j * 512:(j + 1) * 512],
                                         start=(g_ == 0 and j == 0),
                                         stop=(g_ == npairs - 1 and j == 1))
                            t.inc_last(sQP.h, 1)
                        pv_inc[g_] = sQP.n
                sel_done = sQP.n
                @on("vector")
                def _(v, w=sel_done):
                    v.wait_ge(sQP.h, w)
                    v.tensor_copy(selsb[:], pP[0:65, :]).then_inc(sQD.h, 1)
                sQD.inc()
                sel_evac = sQD.n
                gate_exp = [0, 0, 0]
                csw_free = [rqe, rqe]
                exsw_free = rqp
                sw_exp_last = 0
                for i, qt in enumerate(qts):
                    if i < 3:
                        b = i
                        pwg = sQA.inc()
                        @on("tensor")
                        def _(t, b=b, qsl=qsl, QA=QA, se=exp_inc[npairs - 1],
                              ge=gate_exp[b - 1] if b else 0):
                            t.wait_ge(sQE.h, max(se, ge))
                            t.matmul(pS[0][0:1, :], wg98[:, b:b + 1], QA[:, qsl],
                                     start=True, stop=True).then_inc(sQA.h, 1)
                        sQE.inc()
                        @on("scalar")
                        def _(a, b=b, w=pwg, rdv=rdv):
                            a.wait_ge(sQA.h, w)
                            a.wait_ge(sQD.h, rdv)
                            a.activation(g3r[b][:], pS[0][0:1, :], AF.Exp, scale=-1.0
                                         ).then_inc(sQE.h, 1)
                        gate_exp[b] = sQE.n
                    kts = [k for k in range(qt - 4, qt + 1) if k >= 0]
                    nk = len(kts)
                    for j, kt in enumerate(kts):
                        pw = sQA.inc()
                        @on("tensor")
                        def _(t, j=j, kt=kt, qt=qt, bf=bank_free[j % 2], QA=QA):
                            t.wait_ge(sQD.h if bf[0] == "D" else sQE.h, bf[1])
                            t.matmul(pT[j % 2][:], ksw[:, kt * 128:(kt + 1) * 128],
                                     QA[:, qt * 128:(qt + 1) * 128],
                                     start=True, stop=True).then_inc(sQA.h, 1)
                        if kt == qt or kt == qt - 4:
                            slot = 1 if kt == qt else 0
                            @on("vector")
                            def _(v, j=j, kt=kt, qt=qt, slot=slot, w=pw,
                                  cf=csw_free[slot]):
                                v.wait_ge(sQA.h, w)
                                v.wait_ge(sQE.h, cf)
                                v.tensor_add(cswsb[:, slot * 128:(slot + 1) * 128],
                                             pT[j % 2][:],
                                             causadd[:] if kt == qt else wineadd[:])
                                v.inc_last(sQD.h, 1)
                            sQD.inc()
                            mw = sQD.n
                            sQE.inc()
                            @on("scalar")
                            def _(a, j=j, slot=slot, w=mw, xf=exsw_free):
                                a.wait_ge(sQD.h, w)
                                if xf:
                                    a.wait_ge(sQP.h, xf)
                                a.activation(exsw[:, j * 128:(j + 1) * 128],
                                             cswsb[:, slot * 128:(slot + 1) * 128],
                                             AF.Exp).then_inc(sQE.h, 1)
                            csw_free[slot] = sQE.n
                        else:
                            sQE.inc()
                            @on("scalar")
                            def _(a, j=j, w=pw, xf=exsw_free):
                                a.wait_ge(sQA.h, w)
                                if xf:
                                    a.wait_ge(sQP.h, xf)
                                a.activation(exsw[:, j * 128:(j + 1) * 128],
                                             pT[j % 2][:], AF.Exp).then_inc(sQE.h, 1)
                        bank_free[j % 2] = ("E", sQE.n)
                        sw_exp_last = sQE.n
                    pv = sQP.inc()
                    @on("tensor")
                    def _(t, i=i, kts=kts, nk=nk, w=sw_exp_last, es=esink_ready,
                          evw=swT_free):
                        t.wait_ge(sQE.h, w)
                        t.wait_ge(sAC.h, es)
                        if evw:
                            t.wait_ge(sQD.h, evw)
                        t.matmul(pT[2][0:65, :], e65[:],
                                 esink[0:1, i * 128:(i + 1) * 128],
                                 start=True, stop=False)
                        for j, kt in enumerate(kts):
                            t.matmul(pT[2][0:65, :], vones[:, kt * 65:(kt + 1) * 65],
                                     exsw[:, j * 128:(j + 1) * 128],
                                     start=False, stop=(j == nk - 1))
                        t.inc_last(sQP.h, 1)
                    exsw_free = sQP.n
                    @on("vector")
                    def _(v, i=i, w=pv):
                        v.wait_ge(sQP.h, w)
                        v.tensor_copy(swsb[:, i * 128:(i + 1) * 128], pT[2][0:65, :]
                                      ).then_inc(sQD.h, 1)
                    sQD.inc()
                    swT_free = sQD.n
                @on("vector")
                def _(v, qc=qc, wg=gate_exp[2]):
                    v.wait_ge(sQE.h, wg)
                    for b in range(3):
                        v.tensor_scalar_add(g3r[b][:], g3r[b][:], 1.0)
                        v.reciprocal(g3r[b][:], g3r[b][:])
                    v.tensor_add(gnr[:], g3r[0][:], g3r[1][:])
                    v.tensor_add(gnr[:], gnr[:], g3r[2][:])
                    v.tensor_scalar_max(gnr[:], gnr[:], 1e-6)
                    v.reciprocal(gnr[:], gnr[:])
                    v.reciprocal(t1x[:], compsb[64:65, :])
                    v.tensor_mul(g3r[0][:], g3r[0][:], t1x[:])
                    v.tensor_mul(g3r[0][:], g3r[0][:], gnr[:])
                    if qc == 0:
                        v.tensor_mul(g3r[0][0:1, 0:128], g3r[0][0:1, 0:128], z31r[:])
                    v.reciprocal(t1x[:], selsb[64:65, :])
                    v.tensor_mul(g3r[1][:], g3r[1][:], t1x[:])
                    v.tensor_mul(g3r[1][:], g3r[1][:], gnr[:])
                    v.reciprocal(t1x[:], swsb[64:65, :])
                    v.tensor_mul(g3r[2][:], g3r[2][:], t1x[:])
                    v.tensor_mul(g3r[2][:], g3r[2][:], gnr[:])
                    for b in range(3):
                        v.tensor_copy(g3rb[b][:], g3r[b][:])
                    v.inc_last(sQD.h, 1)
                sQD.inc()
                fac_done = sQD.n
                pwb = sQA.inc()
                @on("tensor")
                def _(t, w=fac_done):
                    t.wait_ge(sQD.h, w)
                    t.matmul(pS[1][0:64, :], onesr64[:], g3rb[0][:], start=True, stop=True)
                    t.matmul(pS[2][0:64, :], onesr64[:], g3rb[1][:], start=True, stop=True)
                    t.matmul(pS[3][0:64, :], onesr64[:], g3rb[2][:], start=True, stop=True)
                    t.inc_last(sQA.h, 1)
                @on("vector")
                def _(v, hl=hl, qsl=qsl, w=pwb):
                    v.wait_ge(sQA.h, w)
                    v.tensor_mul(csc[0][0:64, :], compsb[0:64, :], pS[1][0:64, :])
                    v.tensor_mul(csc[1][0:64, :], selsb[0:64, :], pS[2][0:64, :])
                    v.tensor_add(csc[0][0:64, :], csc[0][0:64, :], csc[1][0:64, :])
                    v.tensor_mul(csc[1][0:64, :], swsb[0:64, :], pS[3][0:64, :])
                    v.tensor_add(mxts[hl * 64:(hl + 1) * 64, qsl],
                                 csc[0][0:64, :], csc[1][0:64, :])
                    v.inc_last(sQD.h, 1)
                sQD.inc()
        barrier()
        AT.close()
        if "mx" in dbg:
            t = dout("mx", [128, SEQ], BF16)
            sOD.inc(16)
            @on("sync")
            def _(s, t=t):
                s.dma_start(out=t, in_=mxts[:]).then_inc(sOD.h, 16)
        _chk("ATT")

        sOD.inc(16)
        @on("sync")
        def _(s):
            s.dma_start(out=G2C.rearrange("(r h j) c -> h r j c", r=4, j=16),
                        in_=mxts[:].rearrange("h (r j c) -> h r j c", r=4, c=128)
                        ).then_inc(sOD.h, 16)
        gw = sGP.inc()
        @on("gpsimd")
        def _(g, w=sOD.n):
            g.wait_ge(sOD.h, w)
            g.collective_compute("AllGather", ALU.bypass, replica_groups=RG,
                                 ins=[G2C.opt()], outs=[G2OUT.opt()]).then_inc(sGP.h, 1)
        OP = ExitStack()
        _scopes.append(OP)
        mxt_p = [sbuf(OP, f"mxtp{p}", [128, TOK], BF16) for p in range(4)]
        woutb = sbuf(OP, "woutb", [128, 8 * DM], BF16)
        sGP.inc(16)
        @on("gpsimd")
        def _(g):
            g.dma_start(out=woutb[:].rearrange("p (a q) -> p a q", q=DM),
                        in_=WOUT.rearrange("(a p) q -> p a q", p=128)
                        ).then_inc(sGP.h, 16)
        gw2 = sGP.n
        for p in range(4):
            sGP.inc(16)
            @on("gpsimd")
            def _(g, p=p, w=gw2):
                g.wait_ge(sGP.h, w)
                g.indirect_dma_start(
                    out=mxt_p[p][:], out_offset=None,
                    in_=G2OUT.rearrange("(a j) c -> a (j c)", j=16),
                    in_offset=IndirectOffsetOnAxis(ap=g2idx[:, p:p + 1], axis=0),
                ).then_inc(sGP.h, 16)
        barrier()

        y1b = sbuf(OP, "y1b", [128, DM], BF16)
        y1t = sbuf(OP, "y1t", [128, DM], BF16)
        x2t = sbuf(OP, "x2t", [128, DM])
        po_y1 = [psum(OP, f"po_y1_{h}", [128, 512]) for h in range(2)]
        po_t = [psum(OP, f"po_t{i}", [128, 128], BF16) for i in range(2)]
        po_y2 = [psum(OP, f"po_y2_{h}", [128, 512]) for h in range(2)]

        for ti in range(NT):
            dvw = sDV.n
            pw = sPE.inc()
            @on("tensor")
            def _(t, ti=ti, w=sGP.n, dvw=dvw):
                t.wait_ge(sGP.h, w)
                if ti > 0:
                    t.wait_ge(sDV.h, dvw)
                for gi in range(2):
                    for pp in range(2):
                        p = gi * 2 + pp
                        t.matmul(po_y1[gi][:],
                                 mxt_p[p][:, ti * 128:(ti + 1) * 128],
                                 wogb[:, p * 512:(p + 1) * 512],
                                 start=(pp == 0), stop=(pp == 1))
                t.inc_last(sPE.h, 1)
            @on("vector")
            def _(v, w=pw):
                v.wait_ge(sPE.h, w)
                v.tensor_copy(y1b[:, 0:512], po_y1[0][:])
                v.tensor_copy(y1b[:, 512:1024], po_y1[1][:]).then_inc(sDV.h, 1)
            sDV.inc()
            w2 = sDV.n
            ev = sDV.inc(8)
            pw = sPE.inc(8)
            @on("tensor")
            def _(t, w2=w2, ev=ev):
                t.wait_ge(sDV.h, w2)
                for k in range(8):
                    if k >= 2:
                        t.wait_ge(sDV.h, ev - 7 + (k - 2))
                    t.transpose(po_t[k % 2][:], y1b[:, k * 128:(k + 1) * 128], ident[:])
                    t.inc_last(sPE.h, 1)
            @on("vector")
            def _(v, w=pw):
                for k in range(8):
                    v.wait_ge(sPE.h, w - 7 + k)
                    v.tensor_copy(y1t[:, k * 128:(k + 1) * 128], po_t[k % 2][:])
                    v.inc_last(sDV.h, 1)
            w2 = sDV.n
            pw = sPE.inc()
            @on("tensor")
            def _(t, w2=w2):
                t.wait_ge(sDV.h, w2)
                for half in range(2):
                    for k in range(8):
                        t.matmul(po_y2[half][:], y1t[:, k * 128:(k + 1) * 128],
                                 woutb[:, k * DM + half * 512:k * DM + (half + 1) * 512],
                                 start=(k == 0), stop=(k == 7))
                t.inc_last(sPE.h, 1)
            dvw = sDV.n
            w = sIN.inc(16)
            @on("sync")
            def _(s, ti=ti, dvw=dvw):
                s.wait_ge(sDV.h, dvw)
                s.dma_start(out=x0t[1][:], in_=X1D[ti * 128:(ti + 1) * 128, :]
                            ).then_inc(sIN.h, 16)
            @on("vector")
            def _(v, w=w, pw=pw):
                v.wait_ge(sIN.h, w)
                v.wait_ge(sPE.h, pw)
                v.tensor_add(x2t[:, 0:512], po_y2[0][:], x0t[1][:, 0:512])
                v.tensor_add(x2t[:, 512:1024], po_y2[1][:], x0t[1][:, 512:1024]
                             ).then_inc(sDV.h, 1)
            sDV.inc()
            w2 = sDV.n
            sOD.inc(16)
            @on("sync")
            def _(s, ti=ti, w2=w2):
                s.wait_ge(sDV.h, w2)
                s.dma_start(out=X2D[ti * 128:(ti + 1) * 128, :], in_=x2t[:]
                            ).then_inc(sOD.h, 16)
        barrier()
        OP.close()
        A.close()
        _chk("OP")

        if "x2" in dbg:
            t = dout("x2", [TOK, DM])
            sOD.inc(16)
            @on("sync")
            def _(s, t=t):
                s.dma_start(out=t, in_=X2D[:, :]).then_inc(sOD.h, 16)

        ffn_call("f2", X2D, OUT)
    try:
        _main()
    except _Done:
        pass


    @on("sync")
    def _(s, w=sOD.n):
        s.wait_ge(sOD.h, w)

    for _e in ENGS:
        @on(_e)
        def _(eng):
            eng.nop()
    fsems = {e: root.enter_context(nc.semaphore(f"fence_{e}")) for e in ENGS}
    with root, P:
        with nc.Block() as block:
            @block.sync
            def _(eng):
                eng = EngTrack(eng, fsems["sync"])
                for fn in plan["sync"]:
                    fn(eng)
            @block.tensor
            def _(eng):
                eng = EngTrack(eng, fsems["tensor"])
                for fn in plan["tensor"]:
                    fn(eng)
            @block.vector
            def _(eng):
                eng = EngTrack(eng, fsems["vector"])
                for fn in plan["vector"]:
                    fn(eng)
            @block.scalar
            def _(eng):
                eng = EngTrack(eng, fsems["scalar"])
                for fn in plan["scalar"]:
                    fn(eng)
            @block.gpsimd
            def _(eng):
                eng = EngTrack(eng, fsems["gpsimd"])
                for fn in plan["gpsimd"]:
                    fn(eng)
    return nc, dbg_t


_CACHE = {}


def make_in_maps(inputs, cosf, sinf, consts):
    x = np.ascontiguousarray(np.asarray(inputs["L0_attn_h"], np.float32)
                             ).reshape(2 * SEQ, DM)
    wqup = np.asarray(inputs["L0_attn_w_qup"], np.float32)
    sinkl = np.asarray(inputs["L0_attn_sink_logit"], np.float32).reshape(1, 8)
    base = dict(
        WQC=np.ascontiguousarray(np.asarray(inputs["L0_attn_w_qc"], np.float32)),
        WK=np.ascontiguousarray(np.asarray(inputs["L0_attn_w_k"], np.float32)),
        WV=np.ascontiguousarray(np.asarray(inputs["L0_attn_w_v"], np.float32)),
        GQN=np.ascontiguousarray(np.asarray(inputs["L0_attn_g_qnorm"], np.float32
                                            ).reshape(1, HD)),
        GKN=np.ascontiguousarray(np.asarray(inputs["L0_attn_g_knorm"], np.float32
                                            ).reshape(1, HD)),
        WGATE=np.ascontiguousarray(np.asarray(inputs["L0_attn_w_gate"], np.float32)),
        BGATE=np.ascontiguousarray(np.asarray(inputs["L0_attn_b_gate"], np.float32
                                              ).reshape(1, 3)),
        WOG=np.ascontiguousarray(np.asarray(inputs["L0_attn_w_out_group"], np.float32)),
        WOUT=np.ascontiguousarray(np.asarray(inputs["L0_attn_w_out"], np.float32)),
        WGT=np.ascontiguousarray(np.asarray(inputs["L0_ffn_w_gate"], np.float32).T),
        WUT=np.ascontiguousarray(np.asarray(inputs["L0_ffn_w_up"], np.float32).T),
        WDT=np.ascontiguousarray(np.asarray(inputs["L0_ffn_w_down"], np.float32).T),
    )
    base.update(consts)
    in_maps = []
    for c in range(8):
        hp = c % 4
        j0 = hp * 16
        t0 = j0 * 128
        half = 16
        cs = cosf[t0:t0 + TOK].reshape(NT, 128, half).transpose(1, 0, 2
                                                                ).reshape(128, NT * half)
        sn = sinf[t0:t0 + TOK].reshape(NT, 128, half).transpose(1, 0, 2
                                                                ).reshape(128, NT * half)
        idx = np.empty((128, 4), np.int32)
        for p in range(4):
            idx[:, p] = p * 512 + (j0 // 16) * 128 + np.arange(128)
        m = dict(base)
        m["XIN"] = np.ascontiguousarray(x[c * TOK:(c + 1) * TOK])
        m["WQUPM"] = np.ascontiguousarray(wqup[:, hp * 128:(hp + 1) * 128])
        m["MYSINK"] = np.ascontiguousarray(sinkl[:, 2 * hp:2 * hp + 2])
        m["RCOSS"] = np.ascontiguousarray(cs)
        m["RSINS"] = np.ascontiguousarray(sn)
        m["G2IDX"] = idx
        in_maps.append(m)
    return in_maps


def kernel(**inputs):
    consts, cosf, sinf = host_consts()
    if "prog" not in _CACHE:
        _CACHE["prog"] = build_program(DEBUG_OUTS)
    nc, dbg_t = _CACHE["prog"]
    in_maps = make_in_maps(inputs, cosf, sinf, consts)
    res = run_bass_kernel_spmd(nc, in_maps, core_ids=list(range(8)))
    kernel._last_results = res
    out = np.concatenate([np.asarray(res.results[c]["OUT"]) for c in range(8)], axis=0)
    return out.reshape(2, SEQ, DM)
